# version 13 (final)
# baseline (speedup 1.0000x reference)
import contextlib
import numpy as np
import concourse.bass as bass
import concourse.mybir as mybir
from concourse.bass_utils import run_bass_kernel_spmd
from concourse.aot_env import allow_ici_all_to_all

F32 = mybir.dt.float32
BF16 = mybir.dt.bfloat16
I32 = mybir.dt.int32
AF = mybir.ActivationFunctionType
ALU = mybir.AluOpType
AX = mybir.AxisListType

NC = 8
P = 128
B = 4
T = 4096
D = 1024
DI = 2048
HID = 3072
TW = 512
TPC = B * TW
DSL = DI // NC
EPS = 1e-5
INV127 = float(np.float32(1.0 / 127.0))
ENGS = ("tensor", "vector", "scalar", "gpsimd", "sync")
ARN = 47104


class Sem:
    __slots__ = ("h", "v")

    def __init__(self, h):
        self.h = h
        self.v = 0

    @property
    def now(self):
        return (self, self.v)


class PB:

    def __init__(self, nc, es):
        self.nc = nc
        self.q = {e: [] for e in ENGS}
        self.es = es

    def sem(self, name):
        return Sem(self.es.enter_context(self.nc.semaphore(name)))

    def emit(self, eng, fn, wait=(), inc=None, incn=1, dr=False):
        waits = [(s.h, v) for (s, v) in wait]
        if inc is not None:
            inc.v += incn
        ih = inc.h if inc is not None else None

        def replay(E, waits=waits, fn=fn, ih=ih, incn=incn, dr=dr):
            for h, v in waits:
                E.wait_ge(h, v)
            ins = fn(E)
            if ih is not None:
                ins.then_inc(ih, incn)
            if dr:
                E.drain()
        self.q[eng].append(replay)

    def wait(self, eng, *wps):
        waits = [(s.h, v) for (s, v) in wps]

        def replay(E, waits=waits):
            for h, v in waits:
                E.wait_ge(h, v)
        self.q[eng].append(replay)

    def barrier(self, bar, engs=ENGS):
        for e in engs:
            self.emit(e, lambda E: E.nop(), inc=bar)
        tgt = bar.v
        for e in engs:
            self.wait(e, (bar, tgt))

    def dma(self, eng, out, in_, sem, wait=()):
        self.emit(eng, lambda E, o=out, i=in_: E.dma_start(out=o, in_=i),
                  wait=wait, inc=sem, incn=16)

    def replay_into(self, block):
        for e in ENGS:
            ops = self.q[e]

            def body(E, ops=ops):
                for op in ops:
                    op(E)
            getattr(block, e)(body)


MAXPH = 9
DIAG_BITS = 189


def build_program(dbg=0):
    nc = bass.Bass("TRN2", target_bir_lowering=False)
    dk = dict(kind="ExternalOutput") if dbg else {}

    x_h = nc.dram_tensor("x", [B, TW, D], F32, kind="ExternalInput")
    wiT_h = nc.dram_tensor("wiT", [D, 2 * DI], F32, kind="ExternalInput")
    bi128_h = nc.dram_tensor("bi128", [P, 32], F32, kind="ExternalInput")
    woT_h = nc.dram_tensor("woT", [DI, D], F32, kind="ExternalInput")
    bo_row_h = nc.dram_tensor("bo_row", [1, D], F32, kind="ExternalInput")
    dv128_h = nc.dram_tensor("dv128", [P, 16], F32, kind="ExternalInput")
    wgT_h = nc.dram_tensor("wgT", [D, HID], I32, kind="ExternalInput")
    wuT_h = nc.dram_tensor("wuT", [D, HID], I32, kind="ExternalInput")
    wdT_h = nc.dram_tensor("wdT", [HID, D], I32, kind="ExternalInput")
    alog_h = nc.dram_tensor("alog", [P, 32], F32, kind="ExternalInput")
    bm_h = nc.dram_tensor("bm", [P, 32], F32, kind="ExternalInput")
    cm_h = nc.dram_tensor("cm", [P, 32], F32, kind="ExternalInput")
    cw_h = nc.dram_tensor("cw", [P, 8], F32, kind="ExternalInput")
    cb_h = nc.dram_tensor("cb", [P, 2], F32, kind="ExternalInput")
    wdtT_h = nc.dram_tensor("wdtT", [32, DSL], F32, kind="ExternalInput")
    bdt128_h = nc.dram_tensor("bdt128", [P, 2], F32, kind="ExternalInput")
    wy_bd_h = nc.dram_tensor("wy_bd", [8, P, P], F32, kind="ExternalInput")
    wq_bd_h = nc.dram_tensor("wq_bd", [8, P, P], F32, kind="ExternalInput")
    wk_bd_h = nc.dram_tensor("wk_bd", [8, P, P], F32, kind="ExternalInput")
    wb_bd_h = nc.dram_tensor("wb_bd", [8, P, 8], F32, kind="ExternalInput")
    outwT_h = nc.dram_tensor("outwT", [D, D], F32, kind="ExternalInput")
    lng_row_h = nc.dram_tensor("lng_row", [1, D], F32, kind="ExternalInput")
    lnb_row_h = nc.dram_tensor("lnb_row", [1, D], F32, kind="ExternalInput")
    w1_h = nc.dram_tensor("w1", [D, 512], F32, kind="ExternalInput")
    b1128_h = nc.dram_tensor("b1128", [P, 4], F32, kind="ExternalInput")
    w2_h = nc.dram_tensor("w2", [512, 8], F32, kind="ExternalInput")
    b2_row_h = nc.dram_tensor("b2_row", [1, 8], F32, kind="ExternalInput")
    eW_h = nc.dram_tensor("eW", [8, D, D], F32, kind="ExternalInput")
    eb_h = nc.dram_tensor("eb", [8, 1, D], F32, kind="ExternalInput")
    ident_h = nc.dram_tensor("ident128", [P, P], F32, kind="ExternalInput")
    brep2_h = nc.dram_tensor("brep2h", [2, P], F32, kind="ExternalInput")
    brep2t_h = nc.dram_tensor("brep2th", [P, 2], F32, kind="ExternalInput")
    lngrep_h = nc.dram_tensor("lngrep", [P, D], F32, kind="ExternalInput")
    lnbrep_h = nc.dram_tensor("lnbrep", [P, D], F32, kind="ExternalInput")

    out_h = nc.dram_tensor("out", [B, TW, D], F32, kind="ExternalOutput")

    a2a_in_h = nc.dram_tensor("a2a_in", [NC, DSL, TPC], F32)
    a2a_out_h = nc.dram_tensor("a2a_out", [NC, DSL, TPC], F32)
    ag_in_h = nc.dram_tensor("ag_in", [32, TPC], F32)
    ag_out_h = nc.dram_tensor("ag_out", [NC, 32, TPC], F32)
    a2ay_in_h = nc.dram_tensor("a2ay_in", [NC, DSL, TPC], BF16)
    a2ay_out_h = nc.dram_tensor("a2ay_out", [NC, DSL, TPC], BF16)
    if dbg:
        dbg_a2a_in_h = nc.dram_tensor("dbg_a2a_in", [NC, DSL, TPC], F32, **dk)
        dbg_a2a_out_h = nc.dram_tensor("dbg_a2a_out", [NC, DSL, TPC], F32, **dk)
        dbg_ag_in_h = nc.dram_tensor("dbg_ag_in", [32, TPC], F32, **dk)
        dbg_ag_out_h = nc.dram_tensor("dbg_ag_out", [NC, 32, TPC], F32, **dk)
        dbg_a2ay_out_h = nc.dram_tensor("dbg_a2ay_out", [NC, DSL, TPC], BF16, **dk)
        dbg_xnT_h = nc.dram_tensor("dbg_xnT", [P, 8 * TPC], BF16, **dk)
        dbg_st_h = nc.dram_tensor("dbg_st", [P, 96], F32, **dk)
        dbg_b_h = nc.dram_tensor("dbg_b", [5, P, 4096], F32, **dk)
    sz_spill_h = nc.dram_tensor("sz_spill", [DI, TPC], BF16, **dk)
    x1_spill_h = nc.dram_tensor("x1_spill", [16, P, D], F32, **dk)
    h_spill_h = nc.dram_tensor("h_spill", [HID, TPC], BF16)
    xn2sp_h = nc.dram_tensor("xn2sp", [16, P, D], BF16)
    if dbg:
        dbg_g_h = nc.dram_tensor("dbg_g", [P, 16 * D], BF16, kind="ExternalOutput")
        dbg_c_h = nc.dram_tensor("dbg_c", [4, P, D], F32, kind="ExternalOutput")
        dbg_p_h = nc.dram_tensor("dbg_p", [P, 16 * 512], BF16, kind="ExternalOutput")
        dbg_r_h = nc.dram_tensor("dbg_r", [P, 16 * D], BF16, kind="ExternalOutput")
        dbg_sm_h = nc.dram_tensor("dbg_sm", [4, P, 4096], BF16, kind="ExternalOutput")
        dbg_sc_h = nc.dram_tensor("dbg_sc", [P, 256], F32, kind="ExternalOutput")
        dbg_tb_h = nc.dram_tensor("dbg_tb", [P, 2048 + 256], F32, kind="ExternalOutput")
        dbg_cb_h = nc.dram_tensor("dbg_cb", [4, P, D], F32, kind="ExternalOutput")

    es = contextlib.ExitStack()
    with es:
        def sbt(name, shape, dt):
            return es.enter_context(nc.sbuf_tensor(name, shape, dt))
        AR = sbt("AR", [P, ARN], F32)
        ARb = AR[:].bitcast(BF16)

        def fv(off, n):
            return AR[:, off:off + n]

        def bv(off, n):
            return ARb[:, 2 * off:2 * off + n]

        ident_bf = sbt("ident_bf", [P, P], BF16)

        ones_f = sbt("ones_f", [1, P], F32)
        ones_col = sbt("ones_col", [P, 1], F32)
        brep2 = sbt("brep2", [2, P], BF16)
        stats0 = sbt("stats0", [P, 32], F32)
        stats1 = sbt("stats1", [P, 64], F32)
        bi_sb = sbt("bi_sb", [P, 32], F32)
        scanw = sbt("scanw", [P, 128], F32)
        wdtT_bf = sbt("wdtT_bf", [32, DSL], BF16)
        bo_bf = sbt("bo_bf", [1, D], BF16)
        ones_bf = sbt("ones_bf", [1, P], BF16)
        b2_sb = sbt("b2_sb", [1, 8], F32)
        b1_sb = sbt("b1_sb", [P, 4], F32)
        dv_sb = sbt("dv_sb", [P, 16], F32)
        dvbar = sbt("dvbar", [P, 1], F32)
        lng_rep = sbt("lng_rep", [P, D], F32)
        lnb_rep = sbt("lnb_rep", [P, D], F32)
        tiny = sbt("tiny", [P, 64], F32)
        cpool = sbt("cpool", [P, 16], F32)

        psum = [es.enter_context(nc.psum_tensor(f"ps{i}", [P, 512], F32))
                for i in range(6)]
        pst = [es.enter_context(nc.psum_tensor(f"pst{i}", [P, 1024], BF16))
               for i in range(2)]

        pb = PB(nc, es)
        _fc = {}

        def fc(val, _alloc=[0]):
            val = float(val)
            if val not in _fc:
                i = _alloc[0]
                _alloc[0] += 1
                assert i < 16
                _fc[val] = i
                pb.emit("gpsimd",
                        lambda E, i=i, val=val: E.memset(cpool[:, i:i + 1], val))
            return cpool[:, _fc[val]:_fc[val] + 1]

        ld = pb.sem("ld")
        ldg = pb.sem("ldg")
        smm = pb.sem("mm")
        sev = pb.sem("ev")
        sss = pb.sem("ss")
        svv = pb.sem("vv")
        sst = pb.sem("st")
        scc = pb.sem("cc")
        sgg = pb.sem("gg")
        sbar = pb.sem("bar")

        XNT = 0
        PH = 8192
        A_X = [PH, PH + 1024]
        A_JUNK = PH + 2048
        A_XNBF = PH + 3072
        A_XIN = [PH + 3584, PH + 4096]
        A_Z = [PH + 4608, PH + 4864]
        WQ = [PH + 8192, PH + 12288]

        def c_ident(g):
            g.memset(ones_f[:], 1.0)
            g.memset(ones_bf[:], 1.0)
            return g.memset(ones_col[:], 1.0)
        pb.emit("gpsimd", c_ident, inc=scc)
        pb.dma("gpsimd", ident_bf[:], ident_h.ap(), ldg)

        pb.dma("gpsimd", brep2[:], brep2_h.ap(), ldg)

        for dst, src in [(bi_sb[:], bi128_h.ap()),
                         (dv_sb[:], dv128_h.ap()), (b1_sb[:], b1128_h.ap()),
                         (b2_sb[:], b2_row_h.ap()),
                         (scanw[:, 76:108], alog_h.ap()),
                         (scanw[:, 32:64], bm_h.ap()),
                         (scanw[:, 64:72], cw_h.ap()),
                         (scanw[:, 72:74], cb_h.ap()),
                         (scanw[:, 74:76], bdt128_h.ap()),
                         (tiny[:, 0:32], cm_h.ap())]:
            pb.dma("sync", dst, src, ld)
        ld_small = ld.now
        pb.emit("scalar", lambda E: E.activation(
            out=scanw[:, 0:32], in_=scanw[:, 76:108], func=AF.Exp, scale=1.0),
            wait=[ld_small], inc=sss)
        pb.emit("vector", lambda E: E.tensor_scalar(
            out=scanw[:, 0:32], in0=scanw[:, 0:32], scalar1=-1.0, scalar2=None,
            op0=ALU.mult), wait=[sss.now], inc=svv)
        pb.emit("vector", lambda E: E.tensor_mul(
            out=scanw[:, 32:64], in0=scanw[:, 32:64], in1=tiny[:, 0:32]),
            wait=[ld_small], inc=svv)
        pb.dma("gpsimd", wdtT_bf[:], wdtT_h.ap(), ldg)
        pb.dma("gpsimd", bo_bf[:], bo_row_h.ap(), ldg)
        BCD = 45056
        bcrdy_w = svv.now
        for col in range(32):
            ap = scanw[:, 32 + col:33 + col]
            bcast = bass.AP(ap.tensor, ap.offset, [list(ap.ap[0]), [0, P]])
            pb.emit("vector", lambda E, col=col, bcast=bcast: E.tensor_tensor(
                out=bv(BCD + col * 64, 128), in0=ident_bf[:], in1=bcast,
                op=ALU.mult),
                wait=[bcrdy_w] if col == 0 else [], inc=svv)
        bcdiag_rdy = svv.now
        pb.emit("vector", lambda E: E.reduce_sum(
            out=tiny[:, 32:33], in_=dv_sb[:], axis=AX.X), wait=[ld_small],
            inc=svv, dr=True)
        pb.emit("tensor", lambda E: E.matmul(
            psum[0][0:1, 0:1], tiny[:, 32:33], ones_col[:],
            start=True, stop=True), wait=[svv.now, scc.now], inc=smm)
        pb.emit("vector", lambda E: E.tensor_scalar(
            out=tiny[0:1, 33:34], in0=psum[0][0:1, 0:1], scalar1=1.0 / DI,
            scalar2=None, op0=ALU.mult), wait=[smm.now], inc=svv, dr=True)
        pb.emit("tensor", lambda E: E.matmul(
            psum[1][:, 0:1], ones_f[:], tiny[0:1, 33:34],
            start=True, stop=True), wait=[svv.now], inc=smm)
        pb.emit("vector", lambda E: E.tensor_copy(
            out=dvbar[:], in_=psum[1][:, 0:1]), wait=[smm.now], inc=svv)
        pb.dma("sync", lng_rep[:], lngrep_h.ap(), ld)
        pb.dma("sync", lnb_rep[:], lnbrep_h.ap(), ld)

        for v in (EPS, 1.0, INV127):
            fc(v)
        pb.barrier(sbar)

        NT = 16
        xnbf = bv(A_XNBF, D)
        vfree = {}
        tp_ev = {}
        tp_mm_last = {}
        for tt in range(NT):
            if DIAG_BITS & 64:
                pb.barrier(sbar)
            b, sub = tt // 4, tt % 4
            xt = fv(A_X[tt % 2], D)
            w = [(svv, vfree[tt - 2])] if tt >= 2 else []
            pb.dma("sync", xt, x_h.ap()[b, sub * P:(sub + 1) * P, :], ld, wait=w)
            ldv = ld.now
            pb.emit("vector", lambda E, xt=xt: E.reduce_sum(
                out=tiny[:, 34:35], in_=xt, axis=AX.X), wait=[ldv], inc=svv, dr=True)
            pb.emit("vector", lambda E, xt=xt: E.scalar_tensor_tensor(
                out=fv(A_JUNK, D), in0=xt, scalar=1.0, in1=xt,
                op0=ALU.bypass, op1=ALU.mult), inc=svv)
            pb.emit("vector", lambda E: E.reduce_sum(
                out=tiny[:, 35:36], in_=fv(A_JUNK, D), axis=AX.X), inc=svv, dr=True)
            pb.emit("vector", lambda E, tt=tt: E.tensor_scalar(
                out=stats0[:, 2 * tt:2 * tt + 1], in0=tiny[:, 34:35],
                scalar1=1.0 / D, scalar2=None, op0=ALU.mult), inc=svv, dr=True)
            pb.emit("vector", lambda E: E.tensor_scalar(
                out=tiny[:, 37:38], in0=tiny[:, 35:36], scalar1=1.0 / D,
                scalar2=None, op0=ALU.mult), inc=svv, dr=True)
            pb.emit("vector", lambda E, tt=tt: E.tensor_mul(
                out=tiny[:, 38:39], in0=stats0[:, 2 * tt:2 * tt + 1],
                in1=stats0[:, 2 * tt:2 * tt + 1]), inc=svv, dr=True)
            pb.emit("vector", lambda E: E.tensor_sub(
                out=tiny[:, 38:39], in0=tiny[:, 37:38], in1=tiny[:, 38:39]),
                inc=svv, dr=True)
            vvar = svv.now
            pb.emit("scalar", lambda E: E.activation(
                out=tiny[:, 39:40], in_=tiny[:, 38:39], func=AF.Ln, bias=fc(EPS)),
                wait=[vvar], inc=sss, dr=True)
            pb.emit("scalar", lambda E, tt=tt: E.activation(
                out=stats0[:, 2 * tt + 1:2 * tt + 2], in_=tiny[:, 39:40],
                func=AF.Exp, scale=-0.5), inc=sss, dr=True)
            srstd = sss.now
            def bcast1(col, n=D):
                ap = stats0[:, col:col + 1]
                return bass.AP(ap.tensor, ap.offset, [list(ap.ap[0]), [0, n]])
            pb.emit("vector", lambda E, xt=xt, tt=tt: E.tensor_sub(
                out=fv(A_JUNK, D), in0=xt, in1=bcast1(2 * tt)),
                wait=[srstd], inc=svv, dr=True)
            wxn = [(smm, tp_mm_last[tt - 1])] if tt >= 1 else []
            pb.emit("vector", lambda E, tt=tt: E.tensor_mul(
                out=xnbf, in0=fv(A_JUNK, D), in1=bcast1(2 * tt + 1)),
                wait=wxn, inc=svv)
            vfree[tt] = svv.v
            vxn = svv.now
            for dblk in range(8):
                bank = dblk % 2
                idx = tt * 8 + dblk
                w = [vxn]
                if idx >= 2:
                    w.append((sev, tp_ev[idx - 2]))
                pb.emit("tensor", lambda E, bank=bank, dblk=dblk: E.transpose(
                    pst[bank][:, 0:P], xnbf[:, dblk * P:(dblk + 1) * P],
                    ident_bf[:]), wait=w, inc=smm)
                tp_mm_last[tt] = smm.v
                pb.emit("vector", lambda E, bank=bank, dblk=dblk, tt=tt:
                        E.tensor_copy(
                            out=ARb[:, 2 * XNT + dblk * TPC + tt * P:
                                    2 * XNT + dblk * TPC + (tt + 1) * P],
                            in_=pst[bank][:, 0:P]), wait=[smm.now], inc=sev)
                tp_ev[idx] = sev.v

        pb.barrier(sbar)
        if dbg:
            pb.dma("sync", dbg_xnT_h.ap(), ARb[:, 0:16384], sst)
            pb.dma("sync", dbg_st_h.ap()[:, 0:32], stats0[:], sst)
            pb.dma("sync", dbg_st_h.ap()[:, 32:96], tiny[:], sst)

        def xnT_k(kb, c0, c1):
            return ARb[:, 2 * XNT + kb * TPC + c0:2 * XNT + kb * TPC + c1]

        it = 0
        evfree = {}
        stdone = {}
        qload = {}
        qdone = {}
        for q in range(4):
            wq = bv(WQ[q % 2], 8192)
            w = []
            if q >= 2:
                w.append((smm, qdone[q - 2]))
            for kb in range(8):
                pb.dma("gpsimd",
                       ARb[:, 2 * WQ[q % 2] + kb * 1024:2 * WQ[q % 2] + (kb + 1) * 1024],
                       wiT_h.ap()[kb * P:(kb + 1) * P, q * 1024:(q + 1) * 1024],
                       ldg, wait=(w if kb == 0 else ()))
            qload[q] = ldg.v
            for mq in range(8):
                m = q * 8 + mq
                if DIAG_BITS & 64:
                    pb.barrier(sbar)
                for tc in range(4):
                    bank = 2 + (it % 4)
                    w = [(ldg, qload[q])]
                    if it >= 4:
                        w.append((sss, evfree[it - 4]))

                    def mmg(E, q=q, mq=mq, tc=tc, bank=bank):
                        for kb in range(8):
                            ins = E.matmul(
                                psum[bank][:],
                                ARb[:, 2 * WQ[q % 2] + kb * 1024 + mq * P:
                                        2 * WQ[q % 2] + kb * 1024 + (mq + 1) * P],
                                xnT_k(kb, tc * 512, (tc + 1) * 512),
                                start=(kb == 0), stop=(kb == 7))
                        return ins
                    pb.emit("tensor", mmg, wait=w, inc=smm)
                    qdone[q] = smm.v
                    buf = it % 2
                    w = [smm.now]
                    if it >= 2:
                        w.append((sst, stdone[it - 2]))
                    if m < 16:
                        xo = A_XIN[buf]
                        pb.emit("scalar", lambda E, m=m, bank=bank, xo=xo:
                                E.activation(out=fv(xo, 512), in_=psum[bank][:],
                                             func=AF.Identity,
                                             bias=bi_sb[:, m:m + 1]),
                                wait=w, inc=sss)
                        evfree[it] = sss.v
                        dest, rb = m // 2, (m % 2) * P
                        pb.dma("sync",
                               a2a_in_h.ap()[dest, rb:rb + P,
                                             tc * 512:(tc + 1) * 512],
                               fv(xo, 512), sst, wait=[sss.now])
                        if m == 0:
                            pb.dma("sync", ag_in_h.ap()[:, tc * 512:(tc + 1) * 512],
                                   fv(xo, 512)[0:32, :], sst)
                    else:
                        zo = A_Z[buf]
                        pb.emit("scalar", lambda E, m=m, bank=bank, zo=zo:
                                E.activation(out=bv(zo, 512), in_=psum[bank][:],
                                             func=AF.Silu, bias=bi_sb[:, m:m + 1]),
                                wait=w, inc=sss)
                        evfree[it] = sss.v
                        zm = m - 16
                        pb.dma("sync",
                               sz_spill_h.ap()[zm * P:(zm + 1) * P,
                                               tc * 512:(tc + 1) * 512],
                               bv(zo, 512), sst, wait=[sss.now])
                    stdone[it] = sst.v
                    it += 1

        pb.barrier(sbar)

        def cc_a2a(E):
            with allow_ici_all_to_all():
                ins = E.collective_compute(
                    "AllToAll", ALU.bypass, replica_groups=[list(range(NC))],
                    ins=[a2a_in_h.ap().opt()], outs=[a2a_out_h.ap().opt()])
            return ins
        pb.emit("gpsimd", cc_a2a, inc=scc)
        pb.emit("gpsimd", lambda E: E.collective_compute(
            "AllGather", ALU.bypass, replica_groups=[list(range(NC))],
            ins=[ag_in_h.ap().opt()], outs=[ag_out_h.ap().opt()]), inc=scc)
        cc_x = scc.now
        if dbg:
            pb.dma("sync", dbg_a2a_in_h.ap(), a2a_in_h.ap(), sst, wait=[cc_x])
            pb.dma("sync", dbg_a2a_out_h.ap(), a2a_out_h.ap(), sst, wait=[cc_x])
            pb.dma("sync", dbg_ag_in_h.ap(), ag_in_h.ap(), sst, wait=[cc_x])
            pb.dma("sync", dbg_ag_out_h.ap(), ag_out_h.ap(), sst, wait=[cc_x])

        def finish_early():
            pb.wait("sync", (sst, sst.v), (scc, scc.v), (svv, svv.v),
                    (sss, sss.v), (smm, smm.v), (sev, sev.v))
            for bb in range(B):
                for sub in range(4):
                    pb.dma("sync", out_h.ap()[bb, sub * P:(sub + 1) * P, :],
                           fv(A_X[0], D), sst)
            pb.wait("sync", sst.now)
            block = es.enter_context(nc.Block())
            pb.replay_into(block)

        if MAXPH < 1:
            finish_early()
            return nc

        B_X32 = PH
        B_XIN = PH + 2048
        B_DT = PH + 6144
        B_U = PH + 10240
        B_W = PH + 14336
        B_BB = PH + 18432
        B_BB2 = [PH + 18432, PH + 2048]
        B_A = [PH + 22528, PH + 26624]
        B_H = [PH + 30720, PH + 32768]
        B_YB = PH + 34816

        pstf = [pst[i][:].bitcast(F32)[:, 0:512] for i in range(2)]
        ybank = [psum[i][:] for i in range(6)] + pstf

        x32bf = bv(B_X32, 4096)[0:32, :]
        yev_done = {}
        ysts = {}
        scanS = {}
        ymmS = {}
        convV = {}
        dtmmT = {}
        dtevG = {}
        dbgst = {}
        bd = 0
        for b_i in range(B):
            w = [cc_x]
            if b_i > 0:
                w.append((smm, dtmmT[b_i - 1]))
            pb.emit("gpsimd", lambda E, b_i=b_i: E.dma_start(
                out=x32bf,
                in_=bass.AP(ag_out_h.ap().tensor, b_i * 512,
                            [[2048, 32], [32 * 2048, 8], [1, 512]])),
                wait=w, inc=ldg, incn=16)
            x32rdy = ldg.now
            for dtile in range(2):
                if DIAG_BITS & 2:
                    pb.barrier(sbar)
                wx = [cc_x]
                if bd > 0:
                    wx.append((svv, convV[bd - 1]))
                if dbg and bd == 1:
                    wx.append((sst, dbgst.get("xin", 0)))
                pb.emit("sync", lambda E, b_i=b_i, dtile=dtile: E.dma_start(
                    out=fv(B_XIN, 4096),
                    in_=bass.AP(a2a_out_h.ap().tensor,
                                dtile * P * TPC + b_i * 512,
                                [[TPC, P], [DSL * TPC, 8], [1, 512]])),
                    wait=wx, inc=ld, incn=16)
                xinrdy = ld.now
                for ch in range(8):
                    bank = ch % 2
                    w = [x32rdy]
                    if bd > 0 and ch < 2:
                        w.append((sev, yev_done[bd - 1]))
                    if ch >= 2:
                        w.append((sss, dtevG[(bd, ch - 2)]))
                    elif bd > 0:
                        w.append((sss, dtevG[(bd - 1, 6 + ch)]))
                    pb.emit("tensor", lambda E, ch=ch, dtile=dtile, bank=bank:
                            E.matmul(psum[bank][:],
                                     wdtT_bf[:, dtile * P:(dtile + 1) * P],
                                     x32bf[:, ch * 512:(ch + 1) * 512],
                                     start=True, stop=True), wait=w, inc=smm)
                    w2 = [smm.now]
                    if bd > 0 and ch == 0:
                        w2.append((sev, scanS[(bd - 1, 15)]))
                        if dbg and bd == 1:
                            w2.append((sst, dbgst.get("all", 0)))
                    pb.emit("scalar", lambda E, ch=ch, dtile=dtile, bank=bank:
                            E.activation(out=fv(B_BB + ch * 512, 512),
                                         in_=psum[bank][:], func=AF.Exp,
                                         bias=scanw[:, 74 + dtile:75 + dtile],
                                         scale=1.0),
                            wait=w2, inc=sss, dr=True)
                    pb.emit("scalar", lambda E, ch=ch: E.activation(
                        out=fv(B_DT + ch * 512, 512), in_=fv(B_BB + ch * 512, 512),
                        func=AF.Ln, bias=fc(1.0)), inc=sss, dr=True)
                    dtevG[(bd, ch)] = sss.v
                dtmmT[b_i] = smm.v
                dtrdy = sss.now
                cw_aps = [scanw[:, 64 + dtile * 4 + k:65 + dtile * 4 + k]
                          for k in range(4)]
                pb.emit("vector", lambda E, c=cw_aps[3]: E.tensor_scalar(
                    out=fv(B_U, 4096), in0=fv(B_XIN, 4096), scalar1=c,
                    scalar2=None, op0=ALU.mult), wait=[xinrdy], inc=svv, dr=True)
                for lag in range(1, 4):
                    pb.emit("vector", lambda E, lag=lag, c=cw_aps[3 - lag]:
                            E.scalar_tensor_tensor(
                                out=fv(B_U + lag, 4096 - lag),
                                in0=fv(B_XIN, 4096 - lag), scalar=c,
                                in1=fv(B_U + lag, 4096 - lag),
                                op0=ALU.mult, op1=ALU.add), inc=svv, dr=True)
                convV[bd] = svv.v
                vconv = svv.now
                pb.emit("scalar", lambda E, dtile=dtile: E.activation(
                    out=fv(B_U, 4096), in_=fv(B_U, 4096), func=AF.Silu,
                    bias=scanw[:, 72 + dtile:73 + dtile]),
                    wait=[vconv, dtrdy], inc=sss, dr=True)
                urdy = sss.now
                pb.emit("vector", lambda E: E.tensor_mul(
                    out=fv(B_W, 4096), in0=fv(B_U, 4096), in1=fv(B_DT, 4096)),
                    wait=[urdy], inc=svv, dr=True)
                wrdy = svv.now
                for n in range(16):
                    par = n % 2
                    col = dtile * 16 + n
                    w = [dtrdy]
                    if n >= 2:
                        w.append((sev, scanS[(bd, n - 2)]))
                    elif bd > 0:
                        w.append((sev, scanS[(bd - 1, 14 + n)]))
                    pb.emit("scalar", lambda E, par=par, col=col: E.activation(
                        out=fv(B_A[par], 4096), in_=fv(B_DT, 4096), func=AF.Exp,
                        scale=scanw[:, col:col + 1]), wait=w, inc=sss)
                    sa_n = sss.now
                    w = [sa_n]
                    if n >= 2:
                        w.append((smm, ymmS[(bd, n - 2)]))
                    elif bd > 0:
                        w.append((smm, ymmS[(bd - 1, 14 + n)]))
                    if n == 0:
                        w.append(wrdy)
                    pb.emit("vector", lambda E, par=par: E.tensor_tensor_scan(
                        out=bv(B_H[par], 4096), data0=fv(B_A[par], 4096),
                        data1=fv(B_W, 4096), initial=0.0,
                        op0=ALU.mult, op1=ALU.add), wait=w, inc=sev)
                    scanS[(bd, n)] = sev.v
                    w = [sev.now]
                    if n == 0 and bd > 0:
                        w.append((sev, yev_done[bd - 1]))

                    dcol = dtile * 16 + n

                    def ymm(E, par=par, n=n, dcol=dcol):
                        for ch in range(8):
                            ins = E.matmul(ybank[ch],
                                           bv(BCD + dcol * 64, 128),
                                           bv(B_H[par] + ch * 256, 512),
                                           start=(n == 0), stop=(n == 15))
                        return ins
                    if bd == 0 and n == 0:
                        w.append(bcdiag_rdy)
                    pb.emit("tensor", ymm, wait=w, inc=smm)
                    ymmS[(bd, n)] = smm.v
                if dbg and bd == 0:
                    pb.dma("sync", dbg_b_h.ap()[0], fv(B_DT, 4096), sst,
                           wait=[dtrdy])
                    pb.dma("sync", dbg_b_h.ap()[1], fv(B_U, 4096), sst,
                           wait=[wrdy])
                    pb.dma("sync", dbg_b_h.ap()[2], fv(B_W, 4096), sst)
                    pb.dma("sync", dbg_b_h.ap()[3], fv(B_XIN, 4096), sst)
                    dbgst["xin"] = sst.v
                    pb.emit("vector", lambda E: E.tensor_copy(
                        out=fv(B_BB, 4096), in_=bv(B_H[1], 4096)),
                        wait=[(sev, scanS[(0, 15)])], inc=sev, dr=True)
                    pb.dma("sync", dbg_b_h.ap()[4], fv(B_BB, 4096), sst,
                           wait=[sev.now])
                    dbgst["all"] = sst.v
                ymm_done = smm.now
                for ch in range(8):
                    w = [ymm_done]
                    if bd > 0 and ch == 0:
                        w.append((sst, ysts[bd - 1]))
                    pb.emit("vector", lambda E, ch=ch: E.tensor_copy(
                        out=bv(B_YB + ch * 256, 512), in_=ybank[ch]),
                        wait=w, inc=sev)
                    pb.emit("sync", lambda E, ch=ch, b_i=b_i, dtile=dtile:
                            E.dma_start(
                                out=a2ay_in_h.ap()[ch, dtile * P:(dtile + 1) * P,
                                                   b_i * 512:(b_i + 1) * 512],
                                in_=bv(B_YB + ch * 256, 512)),
                            wait=[sev.now], inc=sst, incn=16)
                yev_done[bd] = sev.v
                ysts[bd] = sst.v
                bd += 1

        pb.wait("gpsimd", (sst, ysts[7]))
        def cc_y(E):
            with allow_ici_all_to_all():
                return E.collective_compute(
                    "AllToAll", ALU.bypass, replica_groups=[list(range(NC))],
                    ins=[a2ay_in_h.ap().opt()], outs=[a2ay_out_h.ap().opt()])
        pb.emit("gpsimd", cc_y, inc=scc)
        cc_y_done = scc.now
        if dbg:
            pb.dma("sync", dbg_a2ay_out_h.ap(), a2ay_out_h.ap(), sst,
                   wait=[cc_y_done])

        pb.barrier(sbar)
        if MAXPH < 2:
            finish_early()
            return nc

        C_WG = [PH, PH + 6144]
        C_SG = PH + 12288
        C_HT = [PH + 12800, PH + 13056]
        slabldg = {}
        slabmm = {}
        hev = {}
        hst = {}
        it = 0
        for ms in range(4):
            if DIAG_BITS & 4:
                pb.barrier(sbar)
            par = ms % 2
            wq = C_WG[par]
            w = []
            if ms >= 2:
                w.append((smm, slabmm[ms - 2]))
            for kb in range(8):
                pb.emit("gpsimd", lambda E, kb=kb, ms=ms, wq=wq: E.dma_start(
                    out=bv(wq + kb * 384, 768),
                    in_=wgT_h.ap()[kb * P:(kb + 1) * P, ms * 768:(ms + 1) * 768]),
                    wait=(w if kb == 0 else ()), inc=ldg, incn=16)
                pb.emit("gpsimd", lambda E, kb=kb, ms=ms, wq=wq: E.dma_start(
                    out=bv(wq + 3072 + kb * 384, 768),
                    in_=wuT_h.ap()[kb * P:(kb + 1) * P, ms * 768:(ms + 1) * 768]),
                    inc=ldg, incn=16)
            slabrdy = ldg.now
            for q in range(4):
                for mi in range(6):
                    m = ms * 6 + mi
                    bg, bu = 2 + 2 * (it % 2), 3 + 2 * (it % 2)
                    w = [slabrdy]
                    if it >= 2:
                        w.append((sev, hev[it - 2]))

                    def ffmm(E, q=q, mi=mi, wq=wq, bg=bg, bu=bu):
                        for kb in range(8):
                            E.matmul(psum[bg][:],
                                     bv(wq + kb * 384 + mi * 64, 128),
                                     ARb[:, kb * TPC + q * 512:
                                         kb * TPC + (q + 1) * 512],
                                     start=(kb == 0), stop=(kb == 7))
                        for kb in range(8):
                            ins = E.matmul(psum[bu][:],
                                           bv(wq + 3072 + kb * 384 + mi * 64, 128),
                                           ARb[:, kb * TPC + q * 512:
                                               kb * TPC + (q + 1) * 512],
                                           start=(kb == 0), stop=(kb == 7))
                        return ins
                    pb.emit("tensor", ffmm, wait=w, inc=smm)
                    slabmm[ms] = smm.v
                    pb.emit("scalar", lambda E, bg=bg: E.activation(
                        out=fv(C_SG, 512), in_=psum[bg][:], func=AF.Silu,
                        scale=INV127), wait=[smm.now], inc=sss, dr=True)
                    hbuf = C_HT[it % 2]
                    w2 = [sss.now]
                    if it >= 2:
                        w2.append((sst, hst[it - 2]))
                    pb.emit("vector", lambda E, bu=bu, hbuf=hbuf:
                            E.scalar_tensor_tensor(
                                out=bv(hbuf, 512), in0=psum[bu][:],
                                scalar=INV127, in1=fv(C_SG, 512),
                                op0=ALU.mult, op1=ALU.mult),
                            wait=w2, inc=sev)
                    hev[it] = sev.v
                    pb.emit("sync", lambda E, m=m, q=q, hbuf=hbuf: E.dma_start(
                        out=h_spill_h.ap()[m * P:(m + 1) * P,
                                           q * 512:(q + 1) * 512],
                        in_=bv(hbuf, 512)), wait=[sev.now], inc=sst, incn=16)
                    hst[it] = sst.v
                    it += 1
        pb.barrier(sbar)
        if MAXPH < 3:
            finish_early()
            return nc

        gevt = {}
        C_HQ = PH
        C_WDT = PH + 6144
        GSLAB = 38912
        for k in range(24):
            pb.emit("gpsimd", lambda E, k=k: E.dma_start(
                out=bv(C_WDT + k * 512, 1024),
                in_=wdT_h.ap()[k * P:(k + 1) * P, :]), inc=ldg, incn=16)
        wdtrdy = ldg.now
        gev = {}
        for q in range(4):
            if DIAG_BITS & 4:
                pb.barrier(sbar)
            for k in range(24):
                pb.emit("sync", lambda E, q=q, k=k: E.dma_start(
                    out=bv(C_HQ + k * 256, 512),
                    in_=h_spill_h.ap()[k * P:(k + 1) * P, q * 512:(q + 1) * 512]),
                    wait=([(sss, gev[q - 1])] if (q > 0 and k == 0) else ()),
                    inc=ld, incn=16)
            hqrdy = ld.now
            for ti in range(4):
                tt = q * 4 + ti
                b0, b1 = 2 * (ti % 2), 2 * (ti % 2) + 1

                def dmm(E, ti=ti, b0=b0, b1=b1):
                    for k in range(24):
                        E.matmul(psum[b0][:], bv(C_HQ + k * 256 + ti * 64, 128),
                                 bv(C_WDT + k * 512, 512),
                                 start=(k == 0), stop=False)
                        E.matmul(psum[b1][:], bv(C_HQ + k * 256 + ti * 64, 128),
                                 bv(C_WDT + k * 512 + 256, 512),
                                 start=(k == 0), stop=False)
                    E.matmul(psum[b0][:], bv(C_HQ, 0)[0:1, 0:0]
                             if False else psum[b0][:, 0:0], psum[b0][:, 0:0],
                             start=False, stop=True) if False else None
                    return E.nop()
                def dmm2(E, ti=ti, b0=b0, b1=b1):
                    for k in range(24):
                        E.matmul(psum[b0][:], bv(C_HQ + k * 256 + ti * 64, 128),
                                 bv(C_WDT + k * 512, 512),
                                 start=(k == 0), stop=(k == 23))
                        ins = E.matmul(psum[b1][:],
                                       bv(C_HQ + k * 256 + ti * 64, 128),
                                       bv(C_WDT + k * 512 + 256, 512),
                                       start=(k == 0), stop=(k == 23))
                    return ins
                w = [hqrdy, wdtrdy]
                if tt >= 2:
                    w.append((sss, gevt[tt - 2]))
                pb.emit("tensor", dmm2, wait=w, inc=smm)
                pb.emit("scalar", lambda E, tt=tt, b0=b0: E.activation(
                    out=ARb[:, 2 * GSLAB + tt * 1024:2 * GSLAB + tt * 1024 + 512],
                    in_=psum[b0][:], func=AF.Silu, scale=INV127),
                    wait=[smm.now], inc=sss)
                pb.emit("scalar", lambda E, tt=tt, b1=b1: E.activation(
                    out=ARb[:, 2 * GSLAB + tt * 1024 + 512:
                            2 * GSLAB + (tt + 1) * 1024],
                    in_=psum[b1][:], func=AF.Silu, scale=INV127), inc=sss)
                gevt[tt] = sss.v
            gev[q] = sss.v
        if dbg:
            pb.dma("sync", dbg_g_h.ap(), ARb[:, 2 * GSLAB:2 * GSLAB + 16384],
                   sst, wait=[sss.now])

        pb.barrier(sbar)
        if MAXPH < 4:
            finish_early()
            return nc

        tp2_last = {}
        tp2_ev = {}
        xfree = {}
        wommL = {}
        xnsp = {}
        C_PROD = PH
        C_WOT = PH + 4096
        C_X = PH + 12288
        C_XN = PH + 13312
        C_S = PH + 14336
        C_GF = PH + 15360
        C_X1 = PH + 16384
        C_XN2F = PH + 17408
        C_JK = PH + 18432
        C_SZ = PH + 19456
        C_XNBF2 = PH + 19712
        XN2TOK = PH + 20480
        for k in range(16):
            pb.emit("gpsimd", lambda E, k=k: E.dma_start(
                out=bv(C_WOT + k * 512, 1024),
                in_=woT_h.ap()[k * P:(k + 1) * P, :]), inc=ldg, incn=16)
        wotrdy = ldg.now
        prodv = {}
        x1st = {}
        tpc1 = {}
        for tg in range(4):
            for k in range(16):
                sblk, rb = k // 2, (k % 2) * P
                w = [cc_y_done]
                if tg > 0 and k == 0:
                    w.append((svv, prodv[tg - 1]))
                    w.append((smm, wommL[tg - 1]))
                pb.emit("sync", lambda E, k=k, sblk=sblk, rb=rb, tg=tg:
                        E.dma_start(
                            out=bv(C_PROD + k * 256, 512),
                            in_=a2ay_in_h.ap()[0, 0, 0:512] if False else
                            a2ay_out_h.ap()[sblk, rb:rb + P,
                                            tg * 512:(tg + 1) * 512]),
                        wait=w, inc=ld, incn=16)
                wsz = [(svv, svv.v)] if k > 0 else ([] if tg == 0 else
                                                    [(svv, prodv[tg - 1])])
                pb.emit("sync", lambda E, k=k, tg=tg: E.dma_start(
                    out=bv(C_SZ, 512),
                    in_=sz_spill_h.ap()[k * P:(k + 1) * P,
                                        tg * 512:(tg + 1) * 512]),
                    wait=wsz, inc=ld, incn=16)
                pb.emit("vector", lambda E, k=k: E.tensor_mul(
                    out=bv(C_PROD + k * 256, 512), in0=bv(C_PROD + k * 256, 512),
                    in1=bv(C_SZ, 512)), wait=[ld.now], inc=svv, dr=True)
            prodv[tg] = svv.v
            prodrdy = svv.now
            if dbg and tg == 0:
                pb.dma("sync", dbg_p_h.ap(), ARb[:, 2 * C_PROD:2 * C_PROD + 8192],
                       sst, wait=[prodrdy])
            for ti in range(4):
                tt = tg * 4 + ti
                b0, b1 = 2 * (ti % 2), 2 * (ti % 2) + 1
                if DIAG_BITS & 1:
                    pb.barrier(sbar)

                def womm(E, ti=ti, b0=b0, b1=b1):
                    for k in range(16):
                        E.matmul(psum[b0][:], bv(C_PROD + k * 256 + ti * 64, 128),
                                 bv(C_WOT + k * 512, 512),
                                 start=(k == 0), stop=False)
                        E.matmul(psum[b1][:], bv(C_PROD + k * 256 + ti * 64, 128),
                                 bv(C_WOT + k * 512 + 256, 512),
                                 start=(k == 0), stop=False)
                    E.matmul(psum[b0][:], ones_bf[:], bo_bf[:, 0:512],
                             start=False, stop=True)
                    return E.matmul(psum[b1][:], ones_bf[:], bo_bf[:, 512:1024],
                                    start=False, stop=True)
                w = [prodrdy, wotrdy]
                if tt >= 2:
                    w.append((svv, tpc1[tt - 2]))
                pb.emit("tensor", womm, wait=w, inc=smm)
                wommL[tg] = smm.v
                womm_d = smm.now
                bq, sub = tt // 4, tt % 4
                pb.emit("sync", lambda E, bq=bq, sub=sub: E.dma_start(
                    out=fv(C_X, D), in_=x_h.ap()[bq, sub * P:(sub + 1) * P, :]),
                    wait=([(svv, xfree[tt - 1])] if tt >= 1 else ()),
                    inc=ld, incn=16)

                def bcast0(col, n=D):
                    ap = stats0[:, col:col + 1]
                    return bass.AP(ap.tensor, ap.offset, [list(ap.ap[0]), [0, n]])
                pb.emit("vector", lambda E, tt=tt: E.tensor_sub(
                    out=fv(C_XN, D), in0=fv(C_X, D), in1=bcast0(2 * tt)),
                    wait=[ld.now], inc=svv, dr=True)
                pb.emit("vector", lambda E, tt=tt: E.tensor_mul(
                    out=fv(C_XN, D), in0=fv(C_XN, D), in1=bcast0(2 * tt + 1)),
                    inc=svv, dr=True)
                pb.emit("vector", lambda E, b0=b0: E.scalar_tensor_tensor(
                    out=fv(C_S, 512), in0=fv(C_XN, 512), scalar=dvbar[:, 0:1],
                    in1=psum[b0][:], op0=ALU.mult, op1=ALU.add),
                    wait=[womm_d], inc=svv)
                pb.emit("vector", lambda E, b1=b1: E.scalar_tensor_tensor(
                    out=fv(C_S + 512, 512), in0=fv(C_XN + 512, 512),
                    scalar=dvbar[:, 0:1], in1=psum[b1][:],
                    op0=ALU.mult, op1=ALU.add), inc=svv, dr=True)
                if dbg and tt == 1:
                    pb.emit("vector", lambda E, b0=b0: E.tensor_copy(
                        out=fv(C_JK, 512), in_=psum[b0][:]), inc=svv, dr=True)
                    pb.emit("vector", lambda E, b1=b1: E.tensor_copy(
                        out=fv(C_JK + 512, 512), in_=psum[b1][:]),
                        inc=svv, dr=True)
                    pb.dma("sync", dbg_c_h.ap()[2], fv(C_JK, D), sst,
                           wait=[svv.now])
                    pb.dma("sync", dbg_c_h.ap()[1], fv(C_S, D), sst)
                tpc1[tt] = svv.v
                pb.emit("vector", lambda E, tt=tt: E.tensor_copy(
                    out=fv(C_GF, D),
                    in_=ARb[:, 2 * GSLAB + tt * 1024:2 * GSLAB + (tt + 1) * 1024]),
                    inc=svv, dr=True)
                pb.emit("vector", lambda E: E.tensor_mul(
                    out=fv(C_S, D), in0=fv(C_S, D), in1=fv(C_GF, D)),
                    inc=svv, dr=True)
                pb.emit("vector", lambda E: E.tensor_add(
                    out=fv(C_X1, D), in0=fv(C_X, D), in1=fv(C_S, D)),
                    inc=svv, dr=True)
                xfree[tt] = svv.v
                x1rdy = svv.now
                if dbg and tt == 1:
                    pb.dma("sync", dbg_c_h.ap()[0], fv(C_XN, D), sst,
                           wait=[x1rdy])
                    pb.dma("sync", dbg_c_h.ap()[3], fv(C_X1, D), sst)
                pb.emit("sync", lambda E, tt=tt: E.dma_start(
                    out=x1_spill_h.ap()[tt], in_=fv(C_X1, D)),
                    wait=[x1rdy], inc=sst, incn=16)
                x1st[tt] = sst.v
                pb.emit("vector", lambda E: E.reduce_sum(
                    out=tiny[:, 34:35], in_=fv(C_X1, D), axis=AX.X),
                    inc=svv, dr=True)
                pb.emit("vector", lambda E: E.scalar_tensor_tensor(
                    out=fv(C_JK, D), in0=fv(C_X1, D), scalar=1.0,
                    in1=fv(C_X1, D), op0=ALU.bypass, op1=ALU.mult),
                    inc=svv, dr=True)
                pb.emit("vector", lambda E: E.reduce_sum(
                    out=tiny[:, 35:36], in_=fv(C_JK, D), axis=AX.X),
                    inc=svv, dr=True)
                pb.emit("vector", lambda E, tt=tt: E.tensor_scalar(
                    out=stats1[:, 4 * tt:4 * tt + 1], in0=tiny[:, 34:35],
                    scalar1=1.0 / D, scalar2=None, op0=ALU.mult),
                    inc=svv, dr=True)
                pb.emit("vector", lambda E: E.tensor_scalar(
                    out=tiny[:, 37:38], in0=tiny[:, 35:36], scalar1=1.0 / D,
                    scalar2=None, op0=ALU.mult), inc=svv, dr=True)
                pb.emit("vector", lambda E, tt=tt: E.tensor_mul(
                    out=tiny[:, 38:39], in0=stats1[:, 4 * tt:4 * tt + 1],
                    in1=stats1[:, 4 * tt:4 * tt + 1]), inc=svv, dr=True)
                pb.emit("vector", lambda E: E.tensor_sub(
                    out=tiny[:, 38:39], in0=tiny[:, 37:38], in1=tiny[:, 38:39]),
                    inc=svv, dr=True)
                vvar2 = svv.now
                pb.emit("scalar", lambda E: E.activation(
                    out=tiny[:, 39:40], in_=tiny[:, 38:39], func=AF.Ln,
                    bias=fc(EPS)), wait=[vvar2], inc=sss, dr=True)
                pb.emit("scalar", lambda E, tt=tt: E.activation(
                    out=stats1[:, 4 * tt + 1:4 * tt + 2], in_=tiny[:, 39:40],
                    func=AF.Exp, scale=-0.5), inc=sss, dr=True)
                rstd2rdy = sss.now

                def bcast2(col, n=D):
                    ap = stats1[:, col:col + 1]
                    return bass.AP(ap.tensor, ap.offset, [list(ap.ap[0]), [0, n]])
                pb.emit("vector", lambda E, tt=tt: E.tensor_sub(
                    out=fv(C_XN2F, D), in0=fv(C_X1, D), in1=bcast2(4 * tt)),
                    wait=[rstd2rdy], inc=svv, dr=True)
                pb.emit("vector", lambda E, tt=tt: E.tensor_mul(
                    out=fv(C_XN2F, D), in0=fv(C_XN2F, D),
                    in1=bcast2(4 * tt + 1)), inc=svv, dr=True)

                pb.emit("vector", lambda E: E.reduce_sum(
                    out=tiny[:, 34:35], in_=fv(C_XN2F, D), axis=AX.X),
                    inc=svv, dr=True)
                pb.emit("vector", lambda E: E.scalar_tensor_tensor(
                    out=fv(C_JK, D), in0=fv(C_XN2F, D), scalar=1.0,
                    in1=fv(C_XN2F, D), op0=ALU.bypass, op1=ALU.mult),
                    inc=svv, dr=True)
                pb.emit("vector", lambda E: E.reduce_sum(
                    out=tiny[:, 35:36], in_=fv(C_JK, D), axis=AX.X),
                    inc=svv, dr=True)
                pb.emit("vector", lambda E: E.tensor_scalar(
                    out=tiny[:, 36:37], in0=tiny[:, 34:35], scalar1=1.0 / D,
                    scalar2=None, op0=ALU.mult), inc=svv, dr=True)
                pb.emit("vector", lambda E: E.tensor_scalar(
                    out=tiny[:, 37:38], in0=tiny[:, 35:36], scalar1=1.0 / D,
                    scalar2=None, op0=ALU.mult), inc=svv, dr=True)
                pb.emit("vector", lambda E: E.tensor_mul(
                    out=tiny[:, 38:39], in0=tiny[:, 36:37], in1=tiny[:, 36:37]),
                    inc=svv, dr=True)
                pb.emit("vector", lambda E: E.tensor_sub(
                    out=tiny[:, 38:39], in0=tiny[:, 37:38], in1=tiny[:, 38:39]),
                    inc=svv, dr=True)
                vvar3 = svv.now
                pb.emit("scalar", lambda E: E.activation(
                    out=tiny[:, 39:40], in_=tiny[:, 38:39], func=AF.Ln,
                    bias=fc(EPS)), wait=[vvar3], inc=sss, dr=True)
                pb.emit("scalar", lambda E: E.activation(
                    out=tiny[:, 40:41], in_=tiny[:, 39:40],
                    func=AF.Exp, scale=-0.5), inc=sss, dr=True)
                r3rdy = sss.now
                pb.emit("vector", lambda E, tt=tt: E.tensor_mul(
                    out=stats1[:, 4 * tt + 2:4 * tt + 3],
                    in0=stats1[:, 4 * tt + 1:4 * tt + 2], in1=tiny[:, 40:41]),
                    wait=[r3rdy], inc=svv, dr=True)
                pb.emit("vector", lambda E, tt=tt: E.tensor_mul(
                    out=tiny[:, 41:42], in0=stats1[:, 4 * tt:4 * tt + 1],
                    in1=stats1[:, 4 * tt + 1:4 * tt + 2]), inc=svv, dr=True)
                pb.emit("vector", lambda E: E.tensor_add(
                    out=tiny[:, 41:42], in0=tiny[:, 41:42], in1=tiny[:, 36:37]),
                    inc=svv, dr=True)
                pb.emit("vector", lambda E: E.tensor_mul(
                    out=tiny[:, 41:42], in0=tiny[:, 41:42], in1=tiny[:, 40:41]),
                    inc=svv, dr=True)
                pb.emit("vector", lambda E, tt=tt: E.tensor_scalar(
                    out=stats1[:, 4 * tt + 3:4 * tt + 4], in0=tiny[:, 41:42],
                    scalar1=-1.0, scalar2=None, op0=ALU.mult), inc=svv, dr=True)
                xnbf2 = bv(C_XNBF2, D)
                wxb = [(smm, tp2_last[tt - 1])] if tt >= 1 else []
                if tt >= 1:
                    wxb.append((sst, xnsp[tt - 1]))
                pb.emit("vector", lambda E, xnbf2=xnbf2: E.tensor_copy(
                    out=xnbf2, in_=fv(C_XN2F, D)),
                    wait=wxb, inc=svv, dr=True)
                vxn2 = svv.now
                pb.emit("sync", lambda E, tt=tt, xnbf2=xnbf2: E.dma_start(
                    out=xn2sp_h.ap()[tt], in_=xnbf2), wait=[vxn2],
                    inc=sst, incn=16)
                xnsp[tt] = sst.v
                for dblk in range(8):
                    bank = dblk % 2
                    idx = tt * 8 + dblk
                    w = [vxn2]
                    if idx >= 2:
                        w.append((sev, tp2_ev[idx - 2]))
                    pb.emit("tensor", lambda E, bank=bank, dblk=dblk, xnbf2=xnbf2:
                            E.transpose(
                        pst[bank][:, 0:P], xnbf2[:, dblk * P:(dblk + 1) * P],
                        ident_bf[:]), wait=w, inc=smm)
                    tp2_last[tt] = smm.v
                    pb.emit("vector", lambda E, bank=bank, dblk=dblk, tt=tt:
                            E.tensor_copy(
                                out=ARb[:, dblk * TPC + tt * P:
                                        dblk * TPC + (tt + 1) * P],
                                in_=pst[bank][:, 0:P]), wait=[smm.now], inc=sev)
                    tp2_ev[idx] = sev.v


        pb.barrier(sbar)
        if MAXPH < 6:
            finish_early()
            return nc

        L_XLO = PH
        L_W1 = PH + 8192
        L_W1L = PH + 10240
        L_H1F = PH + 12288
        L_W2F = PH + 20480
        L_SC = PH + 20512
        L_MK = PH + 20640
        L_W1S = PH + 20768
        L_W1T = PH + 21280
        L_EW = [PH, PH + 4096]
        GS2 = 2 * GSLAB

        ebstage = sbt("ebstage", [1, D], BF16)
        pb.wait("sync", (sbar, sbar.v))
        w1lo = {}
        for kb in range(8):
            pb.dma("sync", fv(L_W1S, 512), w1_h.ap()[kb * P:(kb + 1) * P, :],
                   ld, wait=([(svv, w1lo[kb - 1])] if kb else ()))
            pb.emit("vector", lambda E, kb=kb: E.tensor_copy(
                out=bv(L_W1 + kb * 256, 512), in_=fv(L_W1S, 512)),
                wait=[ld.now], inc=svv, dr=True)
            pb.emit("vector", lambda E, kb=kb: E.tensor_copy(
                out=fv(L_W1T, 512), in_=bv(L_W1 + kb * 256, 512)),
                inc=svv, dr=True)
            pb.emit("vector", lambda E: E.tensor_sub(
                out=fv(L_W1S, 512), in0=fv(L_W1S, 512), in1=fv(L_W1T, 512)),
                inc=svv, dr=True)
            pb.emit("vector", lambda E, kb=kb: E.tensor_copy(
                out=bv(L_W1L + kb * 256, 512), in_=fv(L_W1S, 512)),
                inc=svv, dr=True)
            w1lo[kb] = svv.v
        for kb in range(4):
            pb.dma("sync", fv(L_W2F + kb * 8, 8),
                   w2_h.ap()[kb * P:(kb + 1) * P, :], ld)
        l1wrdy = (svv, svv.v)
        l1wld = ld.now

        XS0, XS1, XS2, XS3 = (L_H1F, L_H1F + 1024, L_H1F + 2048, L_H1F + 3072)
        xlt = {}
        xle = {}
        for tt in range(16):
            w = [(sst, sst.v)]
            if tt >= 1:
                w.append((svv, xlt[tt - 1]))
            pb.dma("sync", fv(XS0, D), x1_spill_h.ap()[tt], ld, wait=w)

            def b2c(col, n=D):
                ap = stats1[:, col:col + 1]
                return bass.AP(ap.tensor, ap.offset, [list(ap.ap[0]), [0, n]])
            pb.emit("vector", lambda E, tt=tt: E.tensor_sub(
                out=fv(XS1, D), in0=fv(XS0, D), in1=b2c(4 * tt)),
                wait=[ld.now], inc=svv, dr=True)
            pb.emit("vector", lambda E, tt=tt: E.tensor_mul(
                out=fv(XS1, D), in0=fv(XS1, D), in1=b2c(4 * tt + 1)),
                inc=svv, dr=True)
            pb.emit("vector", lambda E: E.tensor_copy(
                out=bv(XS2, D), in_=fv(XS1, D)), inc=svv, dr=True)
            pb.emit("vector", lambda E: E.tensor_copy(
                out=fv(XS3, D), in_=bv(XS2, D)), inc=svv, dr=True)
            pb.emit("vector", lambda E: E.tensor_sub(
                out=fv(XS1, D), in0=fv(XS1, D), in1=fv(XS3, D)),
                inc=svv, dr=True)
            pb.emit("vector", lambda E: E.tensor_copy(
                out=bv(XS2, D), in_=fv(XS1, D)), inc=svv, dr=True)
            xlt[tt] = svv.v
            vlo = svv.now
            for dblk in range(8):
                bank = dblk % 2
                pb.emit("tensor", lambda E, bank=bank, dblk=dblk: E.transpose(
                    pst[bank][:, 0:P], bv(XS2 + dblk * 64, 128), ident_bf[:]),
                    wait=[vlo, (sev, max(sev.v - 1, 0))], inc=smm)
                pb.emit("vector", lambda E, bank=bank, dblk=dblk, tt=tt:
                        E.tensor_copy(
                            out=ARb[:, 2 * L_XLO + dblk * TPC + tt * P:
                                    2 * L_XLO + dblk * TPC + (tt + 1) * P],
                            in_=pst[bank][:, 0:P]), wait=[smm.now], inc=sev)
            xle[tt] = sev.v
        xlordy = sev.now

        h1ev = {}
        it = 0
        for m in range(4):
            for tc in range(4):
                bank = it % 4

                def h1mm(E, m=m, tc=tc, bank=bank):
                    for kb in range(8):
                        E.matmul(
                            psum[bank][:], bv(L_W1 + kb * 256 + m * 64, 128),
                            ARb[:, kb * TPC + tc * 512:kb * TPC + (tc + 1) * 512],
                            start=(kb == 0), stop=False)
                    for kb in range(8):
                        E.matmul(
                            psum[bank][:], bv(L_W1L + kb * 256 + m * 64, 128),
                            ARb[:, kb * TPC + tc * 512:kb * TPC + (tc + 1) * 512],
                            start=False, stop=False)
                    for kb in range(8):
                        ins = E.matmul(
                            psum[bank][:], bv(L_W1 + kb * 256 + m * 64, 128),
                            ARb[:, 2 * L_XLO + kb * TPC + tc * 512:
                                2 * L_XLO + kb * TPC + (tc + 1) * 512],
                            start=False, stop=(kb == 7))
                    return ins
                w = [l1wrdy, xlordy]
                if it >= 4:
                    w.append((sss, h1ev[it - 4]))
                pb.emit("tensor", h1mm, wait=w, inc=smm)
                pb.emit("scalar", lambda E, m=m, tc=tc, bank=bank: E.activation(
                    out=fv(L_H1F + m * 2048 + tc * 512, 512), in_=psum[bank][:],
                    func=AF.Relu, bias=b1_sb[:, m:m + 1]),
                    wait=[smm.now], inc=sss)
                h1ev[it] = sss.v
                it += 1
        h1rdy = sss.now

        scev = {}
        for tt in range(16):
            bank = tt % 2

            def scmm(E, tt=tt, bank=bank):
                for mh in range(4):
                    E.matmul(psum[bank][:, 0:8],
                             fv(L_H1F + mh * 2048 + tt * 128, 128),
                             fv(L_W2F + mh * 8, 8),
                             start=(mh == 0), stop=False)
                return E.matmul(psum[bank][:, 0:8], ones_f[:], b2_sb[:],
                                start=False, stop=True)
            w = [h1rdy, l1wld]
            if tt >= 2:
                w.append((svv, scev[tt - 2]))
            pb.emit("tensor", scmm, wait=w, inc=smm)
            pb.emit("vector", lambda E, tt=tt, bank=bank: E.tensor_copy(
                out=fv(L_SC + tt * 8, 8), in_=psum[bank][:, 0:8]),
                wait=[smm.now], inc=svv, dr=True)
            scev[tt] = svv.v
            pb.emit("vector", lambda E, tt=tt: E.reduce_max(
                out=tiny[:, 42:43], in_=fv(L_SC + tt * 8, 8), axis=AX.X),
                inc=svv, dr=True)

            def bct(n=8):
                ap = tiny[:, 42:43]
                return bass.AP(ap.tensor, ap.offset, [list(ap.ap[0]), [0, n]])
            pb.emit("vector", lambda E, tt=tt: E.tensor_tensor(
                out=fv(L_MK + tt * 8, 8), in0=fv(L_SC + tt * 8, 8), in1=bct(),
                op=ALU.is_ge), inc=svv, dr=True)
            pb.emit("vector", lambda E, tt=tt: E.tensor_copy(
                out=tiny[:, 43:44], in_=fv(L_MK + tt * 8, 1)), inc=svv, dr=True)
            for e in range(1, 8):
                pb.emit("vector", lambda E: E.tensor_scalar(
                    out=tiny[:, 44:45], in0=tiny[:, 43:44], scalar1=-1.0,
                    scalar2=1.0, op0=ALU.mult, op1=ALU.add), inc=svv, dr=True)
                pb.emit("vector", lambda E, tt=tt, e=e: E.tensor_mul(
                    out=fv(L_MK + tt * 8 + e, 1), in0=fv(L_MK + tt * 8 + e, 1),
                    in1=tiny[:, 44:45]), inc=svv, dr=True)
                pb.emit("vector", lambda E, tt=tt, e=e: E.tensor_add(
                    out=tiny[:, 43:44], in0=tiny[:, 43:44],
                    in1=fv(L_MK + tt * 8 + e, 1)), inc=svv, dr=True)
        mkrdy = svv.now

        ewld = {}
        exev = {}
        it = 0
        h1mm_done = smm.v
        for e in range(8):
            if DIAG_BITS & 8:
                pb.barrier(sbar)
            buf = L_EW[e % 2]
            w = [(smm, h1mm_done)]
            if e >= 2:
                w.append((smm, ewmm[e - 2]))
            for kb in range(8):
                pb.emit("gpsimd", lambda E, e=e, kb=kb, buf=buf: E.dma_start(
                    out=bv(buf + kb * 512, 1024),
                    in_=eW_h.ap()[e, kb * P:(kb + 1) * P, :]),
                    wait=(w if kb == 0 else ()), inc=ldg, incn=16)
            webias = [(smm, ewmm[e - 1])] if e >= 1 else []
            pb.emit("gpsimd", lambda E, e=e: E.dma_start(
                out=ebstage[:], in_=eb_h.ap()[e]), wait=webias, inc=ldg, incn=16)
            ewld[e] = ldg.now
            if e == 0:
                ewmm = {}
            for tt in range(16):
                b0, b1 = 2 * (it % 2), 2 * (it % 2) + 1

                def exmm(E, e=e, tt=tt, buf=buf, b0=b0, b1=b1):
                    for kb in range(8):
                        E.matmul(psum[b0][:], ARb[:, kb * TPC + tt * P:
                                                  kb * TPC + (tt + 1) * P],
                                 bv(buf + kb * 512, 512),
                                 start=(kb == 0), stop=False)
                        E.matmul(psum[b1][:], ARb[:, kb * TPC + tt * P:
                                                  kb * TPC + (tt + 1) * P],
                                 bv(buf + kb * 512 + 256, 512),
                                 start=(kb == 0), stop=False)
                    E.matmul(psum[b0][:], ones_bf[:], ebstage[:, 0:512],
                             start=False, stop=True)
                    return E.matmul(psum[b1][:], ones_bf[:], ebstage[:, 512:1024],
                                    start=False, stop=True)
                w = [ewld[e], mkrdy]
                if it >= 2:
                    w.append((svv, exev[it - 2]))
                pb.emit("tensor", exmm, wait=w, inc=smm)
                ewmm[e] = smm.v
                mk = fv(L_MK + tt * 8 + e, 1)
                if e == 0:
                    pb.emit("vector", lambda E, tt=tt, b0=b0, mk=mk:
                            E.tensor_scalar(
                                out=ARb[:, GS2 + tt * 1024:GS2 + tt * 1024 + 512],
                                in0=psum[b0][:], scalar1=mk, scalar2=None,
                                op0=ALU.mult), wait=[smm.now], inc=svv)
                    pb.emit("vector", lambda E, tt=tt, b1=b1, mk=mk:
                            E.tensor_scalar(
                                out=ARb[:, GS2 + tt * 1024 + 512:
                                        GS2 + (tt + 1) * 1024],
                                in0=psum[b1][:], scalar1=mk, scalar2=None,
                                op0=ALU.mult), inc=svv, dr=True)
                else:
                    pb.emit("vector", lambda E, tt=tt, b0=b0, mk=mk:
                            E.scalar_tensor_tensor(
                                out=ARb[:, GS2 + tt * 1024:GS2 + tt * 1024 + 512],
                                in0=psum[b0][:], scalar=mk,
                                in1=ARb[:, GS2 + tt * 1024:GS2 + tt * 1024 + 512],
                                op0=ALU.mult, op1=ALU.add),
                            wait=[smm.now], inc=svv)
                    pb.emit("vector", lambda E, tt=tt, b1=b1, mk=mk:
                            E.scalar_tensor_tensor(
                                out=ARb[:, GS2 + tt * 1024 + 512:
                                        GS2 + (tt + 1) * 1024],
                                in0=psum[b1][:], scalar=mk,
                                in1=ARb[:, GS2 + tt * 1024 + 512:
                                         GS2 + (tt + 1) * 1024],
                                op0=ALU.mult, op1=ALU.add), inc=svv, dr=True)
                exev[it] = svv.v
                it += 1
        if dbg:
            pb.dma("sync", dbg_r_h.ap(), ARb[:, GS2:GS2 + 16384], sst,
                   wait=[svv.now])
            pb.dma("sync", dbg_sc_h.ap()[:, 0:128], fv(L_SC, 128), sst)
            pb.dma("sync", dbg_sc_h.ap()[:, 128:256], fv(L_MK, 128), sst)
        g2rdy_marks = []
        for tt in range(16):
            pb.emit("scalar", lambda E, tt=tt: E.activation(
                out=ARb[:, GS2 + tt * 1024:GS2 + (tt + 1) * 1024],
                in_=ARb[:, GS2 + tt * 1024:GS2 + (tt + 1) * 1024],
                func=AF.Silu), wait=[svv.now], inc=sss, dr=True)
        g2rdy = sss.now

        pb.barrier(sbar)
        if MAXPH < 7:
            finish_early()
            return nc

        L_KS = [PH, PH + 2048, PH + 4096]
        L_QMK = [PH + 6144, PH + 8192, PH + 10240]
        L_SMQ = PH + 12288
        L_SMK = PH + 14336
        L_YSM = PH + 16384
        L_XD = PH + 18432
        L_XTOK = PH + 20480
        L_T1 = PH + 22528
        L_T2 = PH + 23552
        L_T3 = PH + 24576
        L_XJ = L_T3
        L_RSF = PH + 25600
        L_BJ = PH + 26112
        L_WBDY = PH + 26368
        L_WBDQ = PH + 26880
        SCL = 36864
        A_D = SCL
        TT_D = SCL + 192
        C_D = SCL + 1728
        WSC = SCL + 1920
        wkbd_sb = sbt("wkbd_sb", [P, 8 * P], BF16)
        wbbd_sb = sbt("wbbd_sb", [P, 8 * 8], BF16)
        brep2t = sbt("brep2t", [P, 2], BF16)

        pb.dma("gpsimd", brep2t[:], brep2t_h.ap(), ldg)
        for hp in range(8):
            pb.emit("gpsimd", lambda E, hp=hp: E.dma_start(
                out=bv(L_WBDY + hp * 64, 128), in_=wy_bd_h.ap()[hp]),
                inc=ldg, incn=16)
            pb.emit("gpsimd", lambda E, hp=hp: E.dma_start(
                out=bv(L_WBDQ + hp * 64, 128), in_=wq_bd_h.ap()[hp]),
                inc=ldg, incn=16)
            pb.emit("gpsimd", lambda E, hp=hp: E.dma_start(
                out=wkbd_sb[:, hp * P:(hp + 1) * P], in_=wk_bd_h.ap()[hp]),
                inc=ldg, incn=16)
            pb.emit("gpsimd", lambda E, hp=hp: E.dma_start(
                out=wbbd_sb[:, hp * 8:(hp + 1) * 8], in_=wb_bd_h.ap()[hp]),
                inc=ldg, incn=16)
        wbdrdy = ldg.now

        def scl3(base_col):
            return bass.AP(AR[:].tensor, base_col, [[ARN, P], [16, 4], [1, 16]])

        def bj3(c):
            return bass.AP(AR[:].tensor, L_BJ + c, [[ARN, P], [64, 4], [4, 16]])

        def wbc16(base_col, jt):
            return bass.AP(AR[:].tensor, base_col + jt * 16,
                           [[ARN, P], [1, 16], [0, 64]])

        def hview(off_f32, n=1024):
            return fv(off_f32, n).rearrange("p (h d) -> p h d", h=16)

        def bview_h(off_f32_b16, jt):
            return bv(off_f32_b16 + jt * 512, 1024).rearrange(
                "p (h d) -> p h d", h=16)

        fam_lhs = {
            "y": lambda hp: bv(L_WBDY + hp * 64, 128),
            "q": lambda hp: bv(L_WBDQ + hp * 64, 128),
            "k": lambda hp: wkbd_sb[:, hp * P:(hp + 1) * P],
        }
        fam_ti = {"y": 0, "q": 1, "k": 2}
        fam_dst = {"y": L_YSM, "q": L_SMQ, "k": L_SMK}
        sridx = {(0, 0): 0, (1, 0): 1, (1, 1): 2, (2, 0): 3, (2, 1): 4,
                 (2, 2): 5}

        st_out = {}
        for t in range(4):
            if DIAG_BITS & 16:
                pb.barrier(sbar)
            if t > 0:
                pb.wait("tensor", (svv, st_out[t - 1]))
            for jt in range(4):
                w = [(svv, svv.v), (sev, sev.v)] if jt == 0 else []
                pb.emit("sync", lambda E, t=t, jt=jt: E.dma_start(
                    out=bv(L_XTOK + jt * 512, 1024),
                    in_=xn2sp_h.ap()[t * 4 + jt]), wait=w, inc=ld, incn=16)
            xtokrdy = ld.now
            for s in range(t):
                for jt in range(4):
                    tile = t * 4 + jt
                    pb.emit("vector", lambda E, s=s, jt=jt:
                            E.tensor_mul(
                                out=fv(L_T1, 1024),
                                in0=bv(L_KS[s] + jt * 512, 1024),
                                in1=bv(L_XTOK + jt * 512, 1024)),
                            wait=[xtokrdy], inc=svv, dr=True)
                    pb.emit("vector", lambda E, s=s, jt=jt: E.tensor_reduce(
                        out=fv(A_D + s * 64 + jt * 16, 16),
                        in_=hview(L_T1), axis=AX.X, op=ALU.add),
                        inc=svv, dr=True)
            adots = svv.now

            def emit_composite(fi, t):
                for jt in range(4):
                    tile = t * 4 + jt
                    wxi = [xtokrdy]
                    if jt == 0:
                        wxi.append((smm, smm.v))
                    pb.emit("vector", lambda E, jt=jt: E.tensor_copy(
                        out=fv(L_XJ, 1024),
                        in_=bv(L_XTOK + jt * 512, 1024)),
                        wait=wxi, inc=svv, dr=True)
                    for r in range(t):
                        first = True
                        for s in range(r, t):
                            sl = sridx[(s, r)]
                            if first:
                                pb.emit("vector", lambda E, fi=fi, sl=sl, s=s,
                                        jt=jt: E.tensor_mul(
                                            out=fv(WSC + jt * 16, 16),
                                            in0=fv(TT_D + (fi * 6 + sl) * 64 +
                                                   jt * 16, 16),
                                            in1=fv(A_D + s * 64 + jt * 16, 16)),
                                        inc=svv, dr=True)
                                first = False
                            else:
                                pb.emit("vector", lambda E, fi=fi, sl=sl, s=s,
                                        jt=jt: E.tensor_mul(
                                            out=fv(WSC + 64, 16),
                                            in0=fv(TT_D + (fi * 6 + sl) * 64 +
                                                   jt * 16, 16),
                                            in1=fv(A_D + s * 64 + jt * 16, 16)),
                                        inc=svv, dr=True)
                                pb.emit("vector", lambda E, jt=jt: E.tensor_add(
                                    out=fv(WSC + jt * 16, 16),
                                    in0=fv(WSC + jt * 16, 16),
                                    in1=fv(WSC + 64, 16)), inc=svv, dr=True)
                        pb.emit("vector", lambda E, r=r, jt=jt: E.tensor_mul(
                            out=hview(L_T1), in0=bview_h(L_QMK[r], jt),
                            in1=wbc16(WSC, jt)), inc=svv, dr=True)
                        pb.emit("vector", lambda E: E.tensor_add(
                            out=fv(L_XJ, 1024), in0=fv(L_XJ, 1024),
                            in1=fv(L_T1, 1024)), inc=svv, dr=True)
                    pb.emit("vector", lambda E: E.tensor_copy(
                        out=bv(L_T2, 1024), in_=fv(L_XJ, 1024)),
                        inc=svv, dr=True)
                    vxj = svv.now
                    for c in range(8):
                        bank = c % 2
                        pb.emit("tensor", lambda E, c=c, bank=bank: E.transpose(
                            pst[bank][:, 0:P],
                            bv(L_T2 + c * 64, 128), ident_bf[:]),
                            wait=[vxj, (sev, max(sev.v - 1, 0))],
                            inc=smm)
                        pb.emit("vector", lambda E, c=c, jt=jt, bank=bank:
                                E.tensor_copy(
                                    out=bv(L_XD + c * 256 + jt * 64, 128),
                                    in_=pst[bank][:, 0:P]),
                                wait=[smm.now], inc=sev)

            for fname in ("y", "q", "k"):
                if DIAG_BITS & 32:
                    pb.barrier(sbar)
                fi = fam_ti[fname]
                if t > 0:
                    emit_composite(fi, t)
                xdrdy = sev.now
                expd = {}
                for hp in range(8):
                    bank = hp % 2
                    if t > 0:
                        rhs = bv(L_XD + hp * 512, 512)
                    else:
                        rhs = ARb[:, hp * TPC + t * 512:hp * TPC + (t + 1) * 512]
                    w = [wbdrdy, xdrdy, g2rdy]
                    if hp >= 2:
                        w.append((sss, expd[hp - 2]))
                    pb.emit("tensor", lambda E, fname=fname, hp=hp, bank=bank,
                            rhs=rhs: E.matmul(
                                psum[bank][:], fam_lhs[fname](hp), rhs,
                                start=True, stop=True), wait=w, inc=smm)
                    wexp = [smm.now]
                    if hp < 2:
                        wexp.append((svv, svv.v))
                    pb.emit("scalar", lambda E, hp=hp, bank=bank, dst=fam_dst[fname]:
                            E.activation(
                        out=bv(dst + hp * 256, 512), in_=psum[bank][:],
                        func=AF.Exp), wait=wexp, inc=sss, dr=True)
                    expd[hp] = sss.v
                exprdy = sss.now
                dst = fam_dst[fname]
                for ch in range(8):
                    wsum = [exprdy]
                    if ch == 0:
                        wsum.append((svv, svv.v))
                    pb.emit("tensor", lambda E, ch=ch, dst=dst: E.matmul(
                        psum[2][0:2, :], brep2t[:], bv(dst + ch * 256, 512),
                        start=True, stop=True), wait=wsum, inc=smm)
                    pb.emit("vector", lambda E: E.reciprocal(
                        out=fv(L_RSF, 512)[0:2, :],
                        in_=psum[2][0:2, :]), wait=[smm.now], inc=svv, dr=True)
                    pb.emit("vector", lambda E: E.tensor_copy(
                        out=bv(L_T3 + 512, 512)[0:2, :],
                        in_=fv(L_RSF, 512)[0:2, :]), inc=svv, dr=True)
                    pb.emit("tensor", lambda E: E.matmul(
                        psum[3][:], brep2[:],
                        bv(L_T3 + 512, 512)[0:2, :],
                        start=True, stop=True), wait=[svv.now], inc=smm)
                    pb.emit("vector", lambda E, ch=ch, dst=dst: E.tensor_mul(
                        out=bv(dst + ch * 256, 512), in0=bv(dst + ch * 256, 512),
                        in1=psum[3][:]), wait=[smm.now], inc=svv, dr=True)
            smdone = svv.now
            if dbg and t == 0:
                pb.dma("sync", dbg_sm_h.ap()[0], ARb[:, 2 * L_YSM:2 * L_YSM + 4096],
                       sst, wait=[smdone])
                pb.dma("sync", dbg_sm_h.ap()[1], ARb[:, 2 * L_SMQ:2 * L_SMQ + 4096],
                       sst)
                pb.dma("sync", dbg_sm_h.ap()[2], ARb[:, 2 * L_SMK:2 * L_SMK + 4096],
                       sst)
                pb.dma("sync", dbg_sm_h.ap()[3], ARb[:, 2 * L_XTOK:2 * L_XTOK + 4096],
                       sst)

            if t > 0:
                emit_composite(3, t)
            xbrdy = sev.now
            for hp in range(8):
                if t > 0:
                    rhs = bv(L_XD + hp * 512, 512)
                else:
                    rhs = ARb[:, hp * TPC + t * 512:hp * TPC + (t + 1) * 512]
                pb.emit("tensor", lambda E, hp=hp, rhs=rhs: E.matmul(
                    psum[4][0:8, :], wbbd_sb[:, hp * 8:(hp + 1) * 8], rhs,
                    start=True, stop=True), wait=[wbdrdy, xbrdy, svv.now],
                    inc=smm)
                pb.emit("scalar", lambda E: E.activation(
                    out=fv(L_T2, 512)[0:8, :], in_=psum[4][0:8, :],
                    func=AF.Tanh, scale=0.5), wait=[smm.now], inc=sss, dr=True)
                pb.emit("vector", lambda E: E.tensor_scalar(
                    out=fv(L_T2, 512)[0:8, :], in0=fv(L_T2, 512)[0:8, :],
                    scalar1=0.5, scalar2=0.5, op0=ALU.mult, op1=ALU.add),
                    wait=[sss.now], inc=svv, dr=True)
                pb.emit("vector", lambda E: E.tensor_copy(
                    out=bv(L_T3, 512)[0:8, :], in_=fv(L_T2, 512)[0:8, :]),
                    inc=svv, dr=True)
                vbeta = svv.now
                for jt in range(4):
                    pb.emit("tensor", lambda E, jt=jt: E.transpose(
                        pst[1][:, 0:8], bv(L_T3 + jt * 64, 128)[0:8, :],
                        ident_bf[0:8, 0:8]),
                        wait=[vbeta, (sev, sev.v)], inc=smm)
                    pb.emit("vector", lambda E, jt=jt, hp=hp: E.tensor_copy(
                        out=bass.AP(AR[:].tensor, L_BJ + jt * 64 + 8 * hp,
                                    [[ARN, P], [4, 2], [1, 4]]),
                        in_=pst[1][:, 0:8].rearrange("p (a b) -> p a b", a=2)),
                        wait=[smm.now], inc=sev)
            betardy = sev.now

            for ch in range(8):
                pb.emit("vector", lambda E, ch=ch: E.tensor_sub(
                    out=bv(L_SMQ + ch * 256, 512), in0=bv(L_SMQ + ch * 256, 512),
                    in1=bv(L_SMK + ch * 256, 512)),
                    wait=[smdone], inc=svv, dr=True)
            qmkv = svv.now
            if t < 3:
                for src, dstb in ((L_SMK, L_KS[t]), (L_SMQ, L_QMK[t])):
                    for hp in range(8):
                        for jt in range(4):
                            bank = (hp * 4 + jt) % 2
                            pb.emit("tensor", lambda E, src=src, hp=hp, jt=jt,
                                    bank=bank: E.transpose(
                                        pst[bank][:, 0:P],
                                        bv(src + hp * 256 + jt * 64, 128),
                                        ident_bf[:]),
                                    wait=[qmkv, (sev, sev.v - (0 if
                                          (hp * 4 + jt) < 2 else 1))],
                                    inc=smm)
                            pb.emit("vector", lambda E, dstb=dstb, hp=hp, jt=jt,
                                    bank=bank: E.tensor_copy(
                                        out=bass.AP(ARb.tensor,
                                                    2 * dstb + jt * 1024 +
                                                    128 * hp,
                                                    [[2 * ARN, P], [64, 2],
                                                     [1, 64]]),
                                        in_=pst[bank][:, 0:P].rearrange(
                                            "p (a b) -> p a b", a=2)),
                                    wait=[smm.now], inc=sev)
                kqdone = sev.now
                for s in range(t):
                    for jt in range(4):
                        pb.emit("vector", lambda E, s=s, jt=jt, t=t:
                                E.tensor_mul(
                                    out=fv(L_T1, 1024),
                                    in0=bv(L_KS[s] + jt * 512, 1024),
                                    in1=bv(L_QMK[t] + jt * 512, 1024)),
                                wait=[kqdone], inc=svv, dr=True)
                        pb.emit("vector", lambda E, s=s, jt=jt: E.tensor_reduce(
                            out=fv(C_D + s * 64 + jt * 16, 16),
                            in_=hview(L_T1), axis=AX.X, op=ALU.add),
                            inc=svv, dr=True)
                for fi in range(4):
                    for r in range(t):
                        first = True
                        for s in range(r, t):
                            sl = sridx[(s, r)]
                            if first:
                                pb.emit("vector", lambda E, fi=fi, sl=sl, s=s:
                                        E.tensor_mul(
                                            out=scl3(WSC),
                                            in0=scl3(TT_D + (fi * 6 + sl) * 64),
                                            in1=scl3(C_D + s * 64)),
                                        wait=[betardy], inc=svv, dr=True)
                                first = False
                            else:
                                pb.emit("vector", lambda E, fi=fi, sl=sl, s=s:
                                        E.tensor_mul(
                                            out=scl3(WSC + 64),
                                            in0=scl3(TT_D + (fi * 6 + sl) * 64),
                                            in1=scl3(C_D + s * 64)),
                                        inc=svv, dr=True)
                                pb.emit("vector", lambda E: E.tensor_add(
                                    out=scl3(WSC), in0=scl3(WSC),
                                    in1=scl3(WSC + 64)), inc=svv, dr=True)
                        sl = sridx[(t, r)]
                        pb.emit("vector", lambda E, fi=fi, sl=sl: E.tensor_mul(
                            out=scl3(TT_D + (fi * 6 + sl) * 64),
                            in0=scl3(WSC), in1=bj3(fi)),
                            inc=svv, dr=True)
                    sl = sridx[(t, t)]
                    pb.emit("vector", lambda E, fi=fi, sl=sl: E.tensor_copy(
                        out=scl3(TT_D + (fi * 6 + sl) * 64), in_=bj3(fi)),
                        wait=[betardy], inc=svv, dr=True)

            if dbg and t == 0:
                pb.dma("sync", dbg_tb_h.ap()[:, 0:2048], fv(SCL, 2048), sst,
                       wait=[(svv, svv.v), (sev, sev.v)])
                pb.dma("sync", dbg_tb_h.ap()[:, 2048:2304], fv(L_BJ, 256), sst)

            allb = [psum[i][:] for i in range(6)] + [pstf[0], pstf[1]]
            owl = {}
            for hp in range(8):
                buf = L_XD + (hp % 2) * 512
                w = [smdone]
                if hp == 0:
                    w.append((smm, smm.v))
                if hp >= 2:
                    w.append((smm, owl[hp - 2]))
                pb.emit("gpsimd", lambda E, hp=hp, buf=buf: E.dma_start(
                    out=bv(buf, 1024),
                    in_=outwT_h.ap()[hp * P:(hp + 1) * P, :]),
                    wait=w, inc=ldg, incn=16)
                owrdy = ldg.now

                def owmm(E, hp=hp, buf=buf, t=t):
                    for js in range(4):
                        E.matmul(allb[2 * js],
                                 ARb[:, 2 * L_YSM + hp * 512 + js * P:
                                     2 * L_YSM + hp * 512 + (js + 1) * P],
                                 bv(buf, 512), start=(hp == 0), stop=(hp == 7))
                        ins = E.matmul(allb[2 * js + 1],
                                       ARb[:, 2 * L_YSM + hp * 512 + js * P:
                                           2 * L_YSM + hp * 512 + (js + 1) * P],
                                       bv(buf + 256, 512),
                                       start=(hp == 0), stop=(hp == 7))
                    return ins
                w2_ = [owrdy]
                if hp == 0:
                    w2_.append((sev, sev.v))
                if t > 0 and hp == 0:
                    w2_.append((svv, st_out[t - 1]))
                pb.emit("tensor", owmm, wait=w2_, inc=smm)
                owl[hp] = smm.v
            owdone = smm.now
            for js in range(4):
                tt = t * 4 + js
                pb.emit("sync", lambda E, tt=tt: E.dma_start(
                    out=fv(L_T1, D), in_=x1_spill_h.ap()[tt]),
                    wait=[(svv, svv.v)], inc=ld, incn=16)
                pb.emit("vector", lambda E, tt=tt: E.tensor_scalar(
                    out=fv(L_T2, D), in0=fv(L_T1, D),
                    scalar1=stats1[:, 4 * tt + 2:4 * tt + 3],
                    scalar2=stats1[:, 4 * tt + 3:4 * tt + 4],
                    op0=ALU.mult, op1=ALU.add), wait=[ld.now], inc=svv, dr=True)
                pb.emit("vector", lambda E: E.tensor_mul(
                    out=fv(L_T2, D), in0=fv(L_T2, D), in1=lng_rep[:]),
                    inc=svv, dr=True)
                pb.emit("vector", lambda E: E.tensor_add(
                    out=fv(L_T2, D), in0=fv(L_T2, D), in1=lnb_rep[:]),
                    inc=svv, dr=True)
                if dbg and tt == 0:
                    pb.dma("sync", dbg_cb_h.ap()[0], fv(L_T2, D), sst,
                           wait=[(svv, svv.v)])
                    pb.emit("vector", lambda E: E.tensor_copy(
                        out=fv(L_T3, 512), in_=allb[0]), wait=[owdone],
                        inc=svv, dr=True)
                    pb.emit("vector", lambda E: E.tensor_copy(
                        out=fv(L_T3 + 512, 512), in_=allb[1]), inc=svv, dr=True)
                    pb.dma("sync", dbg_cb_h.ap()[1], fv(L_T3, D), sst,
                           wait=[(svv, svv.v)])
                pb.emit("vector", lambda E, js=js: E.tensor_add(
                    out=fv(L_T3, 512), in0=fv(L_T2, 512), in1=allb[2 * js]),
                    wait=[owdone], inc=svv, dr=True)
                pb.emit("vector", lambda E, js=js: E.tensor_add(
                    out=fv(L_T3 + 512, 512), in0=fv(L_T2 + 512, 512),
                    in1=allb[2 * js + 1]), inc=svv, dr=True)
                pb.emit("vector", lambda E, tt=tt: E.tensor_copy(
                    out=fv(L_T2, D),
                    in_=ARb[:, GS2 + tt * 1024:GS2 + (tt + 1) * 1024]),
                    inc=svv, dr=True)
                pb.emit("vector", lambda E: E.tensor_mul(
                    out=fv(L_T3, D), in0=fv(L_T3, D), in1=fv(L_T2, D)),
                    inc=svv, dr=True)
                pb.emit("vector", lambda E: E.tensor_add(
                    out=fv(L_T2, D), in0=fv(L_T1, D), in1=fv(L_T3, D)),
                    inc=svv, dr=True)
                if dbg and tt == 0:
                    pb.dma("sync", dbg_cb_h.ap()[2], fv(L_T2, D), sst,
                           wait=[(svv, svv.v)])
                vout = svv.now
                bq, sub = tt // 4, tt % 4
                pb.emit("sync", lambda E, bq=bq, sub=sub: E.dma_start(
                    out=out_h.ap()[bq, sub * P:(sub + 1) * P, :], in_=fv(L_T2, D)),
                    wait=[vout], inc=sst, incn=16)
            st_out[t] = svv.v

        pb.wait("sync", (sst, sst.v))

        block = es.enter_context(nc.Block())
        pb.replay_into(block)
    return nc


def host_inputs(inp, core):
    asf = np.ascontiguousarray
    x = np.asarray(inp["L0_attn_x"])
    d = {}
    d["x"] = asf(x[:, core * TW:(core + 1) * TW, :])
    d["wiT"] = asf(np.asarray(inp["L0_attn_Wi"]).T)
    d["bi128"] = asf(np.asarray(inp["L0_attn_bi"]).reshape(32, P).T)
    d["woT"] = asf(np.asarray(inp["L0_attn_Wo"]).T)
    d["bo_row"] = asf(np.asarray(inp["L0_attn_bo"]).reshape(1, D))
    d["dv128"] = asf(np.asarray(inp["L0_attn_Dv"]).reshape(16, P).T)
    d["wgT"] = asf(np.asarray(inp["L0_ffn_w_gate"]).T)
    d["wuT"] = asf(np.asarray(inp["L0_ffn_w_up"]).T)
    d["wdT"] = asf(np.asarray(inp["L0_ffn_w_down"]).T)
    sl = slice(core * DSL, (core + 1) * DSL)

    def r128(a):
        a = np.asarray(a)
        return asf(a[sl].reshape(2, P, -1).transpose(1, 0, 2).reshape(P, -1))
    d["alog"] = r128(inp["L0_attn_A_log"])
    d["bm"] = r128(inp["L0_attn_Bm"])
    d["cm"] = r128(inp["L0_attn_Cm"])
    d["cw"] = r128(np.asarray(inp["L0_attn_conv_w"]).reshape(DI, 4))
    d["cb"] = r128(np.asarray(inp["L0_attn_conv_b"]).reshape(DI, 1))
    d["wdtT"] = asf(np.asarray(inp["L0_attn_Wdt"])[sl].T)
    d["bdt128"] = r128(np.asarray(inp["L0_attn_bdt"]).reshape(DI, 1))
    Wy = np.asarray(inp["L1_attn_W_y"])[0]
    Wq = np.asarray(inp["L1_attn_W_q"])[0]
    Wk = np.asarray(inp["L1_attn_W_k"])[0]
    wb = np.asarray(inp["L1_attn_w_b"])[0]

    def bd(W):
        o = np.zeros((8, P, P), np.float32)
        for hp in range(8):
            o[hp, 0:64, 0:64] = W[2 * hp].T
            o[hp, 64:128, 64:128] = W[2 * hp + 1].T
        return o
    d["wy_bd"] = bd(Wy)
    d["wq_bd"] = bd(Wq)
    d["wk_bd"] = bd(Wk)
    o = np.zeros((8, P, 8), np.float32)
    for hp in range(8):
        o[hp, 0:64, 0:4] = wb[2 * hp]
        o[hp, 64:128, 4:8] = wb[2 * hp + 1]
    d["wb_bd"] = o
    d["outwT"] = asf(np.asarray(inp["L1_attn_out_w"]).T)
    d["lng_row"] = asf(np.asarray(inp["L1_attn_ln_g"]).reshape(1, D))
    d["lnb_row"] = asf(np.asarray(inp["L1_attn_ln_b"]).reshape(1, D))
    d["w1"] = asf(np.asarray(inp["L1_ffn_w1"]))
    d["b1128"] = asf(np.asarray(inp["L1_ffn_b1"]).reshape(4, P).T)
    d["w2"] = asf(np.asarray(inp["L1_ffn_w2"]))
    d["b2_row"] = asf(np.asarray(inp["L1_ffn_b2"]).reshape(1, 8))
    d["eW"] = asf(np.asarray(inp["L1_ffn_expert_W"]))
    d["eb"] = asf(np.asarray(inp["L1_ffn_expert_b"]).reshape(8, 1, D))
    d["ident128"] = np.eye(P, dtype=np.float32)
    br = np.zeros((2, P), np.float32)
    br[0, 0:64] = 1.0
    br[1, 64:128] = 1.0
    d["brep2h"] = br
    d["brep2th"] = asf(br.T)
    d["lngrep"] = asf(np.broadcast_to(
        np.asarray(inp["L1_attn_ln_g"]).reshape(1, D), (P, D)))
    d["lnbrep"] = asf(np.broadcast_to(
        np.asarray(inp["L1_attn_ln_b"]).reshape(1, D), (P, D)))
    return d


def kernel(**inputs):
    nc = build_program(dbg=0)
    in_maps = [host_inputs(inputs, c) for c in range(NC)]
    res = run_bass_kernel_spmd(nc, in_maps, core_ids=list(range(NC)))
    out = np.concatenate([res.results[c]["out"] for c in range(NC)], axis=1)
    return out.astype(np.float32)


# version 14
# speedup vs baseline: 1.0091x; 1.0091x over previous
import contextlib
import numpy as np
import concourse.bass as bass
import concourse.mybir as mybir
from concourse.bass_utils import run_bass_kernel_spmd
from concourse.aot_env import allow_ici_all_to_all

F32 = mybir.dt.float32
BF16 = mybir.dt.bfloat16
I32 = mybir.dt.int32
AF = mybir.ActivationFunctionType
ALU = mybir.AluOpType
AX = mybir.AxisListType

NC = 8
P = 128
B = 4
T = 4096
D = 1024
DI = 2048
HID = 3072
TW = 512
TPC = B * TW
DSL = DI // NC
EPS = 1e-5
INV127 = float(np.float32(1.0 / 127.0))
ENGS = ("tensor", "vector", "scalar", "gpsimd", "sync")
ARN = 47104


class Sem:
    __slots__ = ("h", "v")

    def __init__(self, h):
        self.h = h
        self.v = 0

    @property
    def now(self):
        return (self, self.v)


class PB:

    def __init__(self, nc, es):
        self.nc = nc
        self.q = {e: [] for e in ENGS}
        self.es = es

    def sem(self, name):
        return Sem(self.es.enter_context(self.nc.semaphore(name)))

    def emit(self, eng, fn, wait=(), inc=None, incn=1, dr=False):
        waits = [(s.h, v) for (s, v) in wait]
        if inc is not None:
            inc.v += incn
        ih = inc.h if inc is not None else None

        def replay(E, waits=waits, fn=fn, ih=ih, incn=incn, dr=dr):
            for h, v in waits:
                E.wait_ge(h, v)
            ins = fn(E)
            if ih is not None:
                ins.then_inc(ih, incn)
            if dr:
                E.drain()
        self.q[eng].append(replay)

    def wait(self, eng, *wps):
        waits = [(s.h, v) for (s, v) in wps]

        def replay(E, waits=waits):
            for h, v in waits:
                E.wait_ge(h, v)
        self.q[eng].append(replay)

    def barrier(self, bar, engs=ENGS):
        for e in engs:
            self.emit(e, lambda E: E.nop(), inc=bar)
        tgt = bar.v
        for e in engs:
            self.wait(e, (bar, tgt))

    def dma(self, eng, out, in_, sem, wait=()):
        self.emit(eng, lambda E, o=out, i=in_: E.dma_start(out=o, in_=i),
                  wait=wait, inc=sem, incn=16)

    def replay_into(self, block):
        for e in ENGS:
            ops = self.q[e]

            def body(E, ops=ops):
                for op in ops:
                    op(E)
            getattr(block, e)(body)


MAXPH = 9
DIAG_BITS = 189


def build_program(dbg=0):
    nc = bass.Bass("TRN2", target_bir_lowering=False)
    dk = dict(kind="ExternalOutput") if dbg else {}

    x_h = nc.dram_tensor("x", [B, TW, D], F32, kind="ExternalInput")
    wiT_h = nc.dram_tensor("wiT", [D, 2 * DI], F32, kind="ExternalInput")
    bi128_h = nc.dram_tensor("bi128", [P, 32], F32, kind="ExternalInput")
    woT_h = nc.dram_tensor("woT", [DI, D], F32, kind="ExternalInput")
    bo_row_h = nc.dram_tensor("bo_row", [1, D], F32, kind="ExternalInput")
    dv128_h = nc.dram_tensor("dv128", [P, 16], F32, kind="ExternalInput")
    wgT_h = nc.dram_tensor("wgT", [D, HID], I32, kind="ExternalInput")
    wuT_h = nc.dram_tensor("wuT", [D, HID], I32, kind="ExternalInput")
    wdT_h = nc.dram_tensor("wdT", [HID, D], I32, kind="ExternalInput")
    alog_h = nc.dram_tensor("alog", [P, 32], F32, kind="ExternalInput")
    bm_h = nc.dram_tensor("bm", [P, 32], F32, kind="ExternalInput")
    cm_h = nc.dram_tensor("cm", [P, 32], F32, kind="ExternalInput")
    cw_h = nc.dram_tensor("cw", [P, 8], F32, kind="ExternalInput")
    cb_h = nc.dram_tensor("cb", [P, 2], F32, kind="ExternalInput")
    wdtT_h = nc.dram_tensor("wdtT", [32, DSL], F32, kind="ExternalInput")
    bdt128_h = nc.dram_tensor("bdt128", [P, 2], F32, kind="ExternalInput")
    wy_bd_h = nc.dram_tensor("wy_bd", [8, P, P], F32, kind="ExternalInput")
    wq_bd_h = nc.dram_tensor("wq_bd", [8, P, P], F32, kind="ExternalInput")
    wk_bd_h = nc.dram_tensor("wk_bd", [8, P, P], F32, kind="ExternalInput")
    wb_bd_h = nc.dram_tensor("wb_bd", [8, P, 8], F32, kind="ExternalInput")
    outwT_h = nc.dram_tensor("outwT", [D, D], F32, kind="ExternalInput")
    lng_row_h = nc.dram_tensor("lng_row", [1, D], F32, kind="ExternalInput")
    lnb_row_h = nc.dram_tensor("lnb_row", [1, D], F32, kind="ExternalInput")
    w1_h = nc.dram_tensor("w1", [D, 512], F32, kind="ExternalInput")
    b1128_h = nc.dram_tensor("b1128", [P, 4], F32, kind="ExternalInput")
    w2_h = nc.dram_tensor("w2", [512, 8], F32, kind="ExternalInput")
    b2_row_h = nc.dram_tensor("b2_row", [1, 8], F32, kind="ExternalInput")
    eW_h = nc.dram_tensor("eW", [8, D, D], F32, kind="ExternalInput")
    eb_h = nc.dram_tensor("eb", [8, 1, D], F32, kind="ExternalInput")
    ident_h = nc.dram_tensor("ident128", [P, P], F32, kind="ExternalInput")
    brep2_h = nc.dram_tensor("brep2h", [2, P], F32, kind="ExternalInput")
    brep2t_h = nc.dram_tensor("brep2th", [P, 2], F32, kind="ExternalInput")
    lngrep_h = nc.dram_tensor("lngrep", [P, D], F32, kind="ExternalInput")
    lnbrep_h = nc.dram_tensor("lnbrep", [P, D], F32, kind="ExternalInput")

    out_h = nc.dram_tensor("out", [B, TW, D], F32, kind="ExternalOutput")

    a2a_in_h = nc.dram_tensor("a2a_in", [NC, DSL, TPC], F32)
    a2a_out_h = nc.dram_tensor("a2a_out", [NC, DSL, TPC], F32)
    ag_in_h = nc.dram_tensor("ag_in", [32, TPC], F32)
    ag_out_h = nc.dram_tensor("ag_out", [NC, 32, TPC], F32)
    a2ay_in_h = nc.dram_tensor("a2ay_in", [NC, DSL, TPC], BF16)
    a2ay_out_h = nc.dram_tensor("a2ay_out", [NC, DSL, TPC], BF16)
    if dbg:
        dbg_a2a_in_h = nc.dram_tensor("dbg_a2a_in", [NC, DSL, TPC], F32, **dk)
        dbg_a2a_out_h = nc.dram_tensor("dbg_a2a_out", [NC, DSL, TPC], F32, **dk)
        dbg_ag_in_h = nc.dram_tensor("dbg_ag_in", [32, TPC], F32, **dk)
        dbg_ag_out_h = nc.dram_tensor("dbg_ag_out", [NC, 32, TPC], F32, **dk)
        dbg_a2ay_out_h = nc.dram_tensor("dbg_a2ay_out", [NC, DSL, TPC], BF16, **dk)
        dbg_xnT_h = nc.dram_tensor("dbg_xnT", [P, 8 * TPC], BF16, **dk)
        dbg_st_h = nc.dram_tensor("dbg_st", [P, 96], F32, **dk)
        dbg_b_h = nc.dram_tensor("dbg_b", [5, P, 4096], F32, **dk)
    sz_spill_h = nc.dram_tensor("sz_spill", [DI, TPC], BF16, **dk)
    x1_spill_h = nc.dram_tensor("x1_spill", [16, P, D], F32, **dk)
    h_spill_h = nc.dram_tensor("h_spill", [HID, TPC], BF16)
    xn2sp_h = nc.dram_tensor("xn2sp", [16, P, D], BF16)
    if dbg:
        dbg_g_h = nc.dram_tensor("dbg_g", [P, 16 * D], BF16, kind="ExternalOutput")
        dbg_c_h = nc.dram_tensor("dbg_c", [4, P, D], F32, kind="ExternalOutput")
        dbg_p_h = nc.dram_tensor("dbg_p", [P, 16 * 512], BF16, kind="ExternalOutput")
        dbg_r_h = nc.dram_tensor("dbg_r", [P, 16 * D], BF16, kind="ExternalOutput")
        dbg_sm_h = nc.dram_tensor("dbg_sm", [4, P, 4096], BF16, kind="ExternalOutput")
        dbg_sc_h = nc.dram_tensor("dbg_sc", [P, 256], F32, kind="ExternalOutput")
        dbg_tb_h = nc.dram_tensor("dbg_tb", [P, 2048 + 256], F32, kind="ExternalOutput")
        dbg_cb_h = nc.dram_tensor("dbg_cb", [4, P, D], F32, kind="ExternalOutput")

    es = contextlib.ExitStack()
    with es:
        def sbt(name, shape, dt):
            return es.enter_context(nc.sbuf_tensor(name, shape, dt))
        AR = sbt("AR", [P, ARN], F32)
        ARb = AR[:].bitcast(BF16)

        def fv(off, n):
            return AR[:, off:off + n]

        def bv(off, n):
            return ARb[:, 2 * off:2 * off + n]

        ident_bf = sbt("ident_bf", [P, P], BF16)

        ones_f = sbt("ones_f", [1, P], F32)
        ones_col = sbt("ones_col", [P, 1], F32)
        brep2 = sbt("brep2", [2, P], BF16)
        stats0 = sbt("stats0", [P, 32], F32)
        stats1 = sbt("stats1", [P, 64], F32)
        bi_sb = sbt("bi_sb", [P, 32], F32)
        scanw = sbt("scanw", [P, 128], F32)
        wdtT_bf = sbt("wdtT_bf", [32, DSL], BF16)
        bo_bf = sbt("bo_bf", [1, D], BF16)
        ones_bf = sbt("ones_bf", [1, P], BF16)
        b2_sb = sbt("b2_sb", [1, 8], F32)
        b1_sb = sbt("b1_sb", [P, 4], F32)
        dv_sb = sbt("dv_sb", [P, 16], F32)
        dvbar = sbt("dvbar", [P, 1], F32)
        lng_rep = sbt("lng_rep", [P, D], F32)
        lnb_rep = sbt("lnb_rep", [P, D], F32)
        tiny = sbt("tiny", [P, 64], F32)
        cpool = sbt("cpool", [P, 16], F32)

        psum = [es.enter_context(nc.psum_tensor(f"ps{i}", [P, 512], F32))
                for i in range(6)]
        pst = [es.enter_context(nc.psum_tensor(f"pst{i}", [P, 1024], BF16))
               for i in range(2)]

        pb = PB(nc, es)
        _fc = {}

        def fc(val, _alloc=[0]):
            val = float(val)
            if val not in _fc:
                i = _alloc[0]
                _alloc[0] += 1
                assert i < 16
                _fc[val] = i
                pb.emit("gpsimd",
                        lambda E, i=i, val=val: E.memset(cpool[:, i:i + 1], val))
            return cpool[:, _fc[val]:_fc[val] + 1]

        ld = pb.sem("ld")
        ldg = pb.sem("ldg")
        smm = pb.sem("mm")
        sev = pb.sem("ev")
        sss = pb.sem("ss")
        svv = pb.sem("vv")
        sst = pb.sem("st")
        scc = pb.sem("cc")
        sgg = pb.sem("gg")
        sbar = pb.sem("bar")

        XNT = 0
        PH = 8192
        A_X = [PH, PH + 1024]
        A_JUNK = PH + 2048
        A_XNBF = PH + 3072
        A_XIN = [PH + 3584, PH + 4096]
        A_Z = [PH + 4608, PH + 4864]
        WQ = [PH + 8192, PH + 12288]

        def c_ident(g):
            g.memset(ones_f[:], 1.0)
            g.memset(ones_bf[:], 1.0)
            return g.memset(ones_col[:], 1.0)
        pb.emit("gpsimd", c_ident, inc=scc)
        pb.dma("gpsimd", ident_bf[:], ident_h.ap(), ldg)

        pb.dma("gpsimd", brep2[:], brep2_h.ap(), ldg)

        for dst, src in [(bi_sb[:], bi128_h.ap()),
                         (dv_sb[:], dv128_h.ap()), (b1_sb[:], b1128_h.ap()),
                         (b2_sb[:], b2_row_h.ap()),
                         (scanw[:, 76:108], alog_h.ap()),
                         (scanw[:, 32:64], bm_h.ap()),
                         (scanw[:, 64:72], cw_h.ap()),
                         (scanw[:, 72:74], cb_h.ap()),
                         (scanw[:, 74:76], bdt128_h.ap()),
                         (tiny[:, 0:32], cm_h.ap())]:
            pb.dma("sync", dst, src, ld)
        ld_small = ld.now
        pb.emit("scalar", lambda E: E.activation(
            out=scanw[:, 0:32], in_=scanw[:, 76:108], func=AF.Exp, scale=1.0),
            wait=[ld_small], inc=sss)
        pb.emit("vector", lambda E: E.tensor_scalar(
            out=scanw[:, 0:32], in0=scanw[:, 0:32], scalar1=-1.0, scalar2=None,
            op0=ALU.mult), wait=[sss.now], inc=svv)
        pb.emit("vector", lambda E: E.tensor_mul(
            out=scanw[:, 32:64], in0=scanw[:, 32:64], in1=tiny[:, 0:32]),
            wait=[ld_small], inc=svv)
        pb.dma("gpsimd", wdtT_bf[:], wdtT_h.ap(), ldg)
        pb.dma("gpsimd", bo_bf[:], bo_row_h.ap(), ldg)
        BCD = 45056
        bcrdy_w = svv.now
        for col in range(32):
            ap = scanw[:, 32 + col:33 + col]
            bcast = bass.AP(ap.tensor, ap.offset, [list(ap.ap[0]), [0, P]])
            pb.emit("vector", lambda E, col=col, bcast=bcast: E.tensor_tensor(
                out=bv(BCD + col * 64, 128), in0=ident_bf[:], in1=bcast,
                op=ALU.mult),
                wait=[bcrdy_w] if col == 0 else [], inc=svv)
        bcdiag_rdy = svv.now
        pb.emit("vector", lambda E: E.reduce_sum(
            out=tiny[:, 32:33], in_=dv_sb[:], axis=AX.X), wait=[ld_small],
            inc=svv, dr=True)
        pb.emit("tensor", lambda E: E.matmul(
            psum[0][0:1, 0:1], tiny[:, 32:33], ones_col[:],
            start=True, stop=True), wait=[svv.now, scc.now], inc=smm)
        pb.emit("vector", lambda E: E.tensor_scalar(
            out=tiny[0:1, 33:34], in0=psum[0][0:1, 0:1], scalar1=1.0 / DI,
            scalar2=None, op0=ALU.mult), wait=[smm.now], inc=svv, dr=True)
        pb.emit("tensor", lambda E: E.matmul(
            psum[1][:, 0:1], ones_f[:], tiny[0:1, 33:34],
            start=True, stop=True), wait=[svv.now], inc=smm)
        pb.emit("vector", lambda E: E.tensor_copy(
            out=dvbar[:], in_=psum[1][:, 0:1]), wait=[smm.now], inc=svv)
        pb.dma("sync", lng_rep[:], lngrep_h.ap(), ld)
        pb.dma("sync", lnb_rep[:], lnbrep_h.ap(), ld)

        for v in (EPS, 1.0, INV127):
            fc(v)
        pb.barrier(sbar)

        NT = 16
        xnbf = bv(A_XNBF, D)
        vfree = {}
        tp_ev = {}
        tp_mm_last = {}
        for tt in range(NT):
            if DIAG_BITS & 64:
                pb.barrier(sbar)
            b, sub = tt // 4, tt % 4
            xt = fv(A_X[tt % 2], D)
            w = [(svv, vfree[tt - 2])] if tt >= 2 else []
            pb.dma("sync", xt, x_h.ap()[b, sub * P:(sub + 1) * P, :], ld, wait=w)
            ldv = ld.now
            pb.emit("vector", lambda E, xt=xt: E.reduce_sum(
                out=tiny[:, 34:35], in_=xt, axis=AX.X), wait=[ldv], inc=svv, dr=True)
            pb.emit("vector", lambda E, xt=xt: E.scalar_tensor_tensor(
                out=fv(A_JUNK, D), in0=xt, scalar=1.0, in1=xt,
                op0=ALU.bypass, op1=ALU.mult), inc=svv)
            pb.emit("vector", lambda E: E.reduce_sum(
                out=tiny[:, 35:36], in_=fv(A_JUNK, D), axis=AX.X), inc=svv, dr=True)
            pb.emit("vector", lambda E, tt=tt: E.tensor_scalar(
                out=stats0[:, 2 * tt:2 * tt + 1], in0=tiny[:, 34:35],
                scalar1=1.0 / D, scalar2=None, op0=ALU.mult), inc=svv, dr=True)
            pb.emit("vector", lambda E: E.tensor_scalar(
                out=tiny[:, 37:38], in0=tiny[:, 35:36], scalar1=1.0 / D,
                scalar2=None, op0=ALU.mult), inc=svv, dr=True)
            pb.emit("vector", lambda E, tt=tt: E.tensor_mul(
                out=tiny[:, 38:39], in0=stats0[:, 2 * tt:2 * tt + 1],
                in1=stats0[:, 2 * tt:2 * tt + 1]), inc=svv, dr=True)
            pb.emit("vector", lambda E: E.tensor_sub(
                out=tiny[:, 38:39], in0=tiny[:, 37:38], in1=tiny[:, 38:39]),
                inc=svv, dr=True)
            vvar = svv.now
            pb.emit("scalar", lambda E: E.activation(
                out=tiny[:, 39:40], in_=tiny[:, 38:39], func=AF.Ln, bias=fc(EPS)),
                wait=[vvar], inc=sss, dr=True)
            pb.emit("scalar", lambda E, tt=tt: E.activation(
                out=stats0[:, 2 * tt + 1:2 * tt + 2], in_=tiny[:, 39:40],
                func=AF.Exp, scale=-0.5), inc=sss, dr=True)
            srstd = sss.now
            def bcast1(col, n=D):
                ap = stats0[:, col:col + 1]
                return bass.AP(ap.tensor, ap.offset, [list(ap.ap[0]), [0, n]])
            pb.emit("vector", lambda E, xt=xt, tt=tt: E.tensor_sub(
                out=fv(A_JUNK, D), in0=xt, in1=bcast1(2 * tt)),
                wait=[srstd], inc=svv, dr=True)
            wxn = [(smm, tp_mm_last[tt - 1])] if tt >= 1 else []
            pb.emit("vector", lambda E, tt=tt: E.tensor_mul(
                out=xnbf, in0=fv(A_JUNK, D), in1=bcast1(2 * tt + 1)),
                wait=wxn, inc=svv)
            vfree[tt] = svv.v
            vxn = svv.now
            for dblk in range(8):
                bank = dblk % 2
                idx = tt * 8 + dblk
                w = [vxn]
                if idx >= 2:
                    w.append((sev, tp_ev[idx - 2]))
                pb.emit("tensor", lambda E, bank=bank, dblk=dblk: E.transpose(
                    pst[bank][:, 0:P], xnbf[:, dblk * P:(dblk + 1) * P],
                    ident_bf[:]), wait=w, inc=smm)
                tp_mm_last[tt] = smm.v
                pb.emit("vector", lambda E, bank=bank, dblk=dblk, tt=tt:
                        E.tensor_copy(
                            out=ARb[:, 2 * XNT + dblk * TPC + tt * P:
                                    2 * XNT + dblk * TPC + (tt + 1) * P],
                            in_=pst[bank][:, 0:P]), wait=[smm.now], inc=sev)
                tp_ev[idx] = sev.v

        pb.barrier(sbar)
        if dbg:
            pb.dma("sync", dbg_xnT_h.ap(), ARb[:, 0:16384], sst)
            pb.dma("sync", dbg_st_h.ap()[:, 0:32], stats0[:], sst)
            pb.dma("sync", dbg_st_h.ap()[:, 32:96], tiny[:], sst)

        def xnT_k(kb, c0, c1):
            return ARb[:, 2 * XNT + kb * TPC + c0:2 * XNT + kb * TPC + c1]

        it = 0
        evfree = {}
        stdone = {}
        qload = {}
        qdone = {}
        for q in range(4):
            wq = bv(WQ[q % 2], 8192)
            w = []
            if q >= 2:
                w.append((smm, qdone[q - 2]))
            for kb in range(8):
                pb.dma("gpsimd",
                       ARb[:, 2 * WQ[q % 2] + kb * 1024:2 * WQ[q % 2] + (kb + 1) * 1024],
                       wiT_h.ap()[kb * P:(kb + 1) * P, q * 1024:(q + 1) * 1024],
                       ldg, wait=(w if kb == 0 else ()))
            qload[q] = ldg.v
            for mq in range(8):
                m = q * 8 + mq
                if DIAG_BITS & 64:
                    pb.barrier(sbar)
                for tc in range(4):
                    bank = 2 + (it % 4)
                    w = [(ldg, qload[q])]
                    if it >= 4:
                        w.append((sss, evfree[it - 4]))

                    def mmg(E, q=q, mq=mq, tc=tc, bank=bank):
                        for kb in range(8):
                            ins = E.matmul(
                                psum[bank][:],
                                ARb[:, 2 * WQ[q % 2] + kb * 1024 + mq * P:
                                        2 * WQ[q % 2] + kb * 1024 + (mq + 1) * P],
                                xnT_k(kb, tc * 512, (tc + 1) * 512),
                                start=(kb == 0), stop=(kb == 7))
                        return ins
                    pb.emit("tensor", mmg, wait=w, inc=smm)
                    qdone[q] = smm.v
                    buf = it % 2
                    w = [smm.now]
                    if it >= 2:
                        w.append((sst, stdone[it - 2]))
                    if m < 16:
                        xo = A_XIN[buf]
                        pb.emit("scalar", lambda E, m=m, bank=bank, xo=xo:
                                E.activation(out=fv(xo, 512), in_=psum[bank][:],
                                             func=AF.Identity,
                                             bias=bi_sb[:, m:m + 1]),
                                wait=w, inc=sss)
                        evfree[it] = sss.v
                        dest, rb = m // 2, (m % 2) * P
                        pb.dma("sync",
                               a2a_in_h.ap()[dest, rb:rb + P,
                                             tc * 512:(tc + 1) * 512],
                               fv(xo, 512), sst, wait=[sss.now])
                        if m == 0:
                            pb.dma("sync", ag_in_h.ap()[:, tc * 512:(tc + 1) * 512],
                                   fv(xo, 512)[0:32, :], sst)
                    else:
                        zo = A_Z[buf]
                        pb.emit("scalar", lambda E, m=m, bank=bank, zo=zo:
                                E.activation(out=bv(zo, 512), in_=psum[bank][:],
                                             func=AF.Silu, bias=bi_sb[:, m:m + 1]),
                                wait=w, inc=sss)
                        evfree[it] = sss.v
                        zm = m - 16
                        pb.dma("sync",
                               sz_spill_h.ap()[zm * P:(zm + 1) * P,
                                               tc * 512:(tc + 1) * 512],
                               bv(zo, 512), sst, wait=[sss.now])
                    stdone[it] = sst.v
                    it += 1

        pb.barrier(sbar)

        def cc_a2a(E):
            with allow_ici_all_to_all():
                ins = E.collective_compute(
                    "AllToAll", ALU.bypass, replica_groups=[list(range(NC))],
                    ins=[a2a_in_h.ap().opt()], outs=[a2a_out_h.ap().opt()])
            return ins
        pb.emit("gpsimd", cc_a2a, inc=scc)
        pb.emit("gpsimd", lambda E: E.collective_compute(
            "AllGather", ALU.bypass, replica_groups=[list(range(NC))],
            ins=[ag_in_h.ap().opt()], outs=[ag_out_h.ap().opt()]), inc=scc)
        cc_x = scc.now
        if dbg:
            pb.dma("sync", dbg_a2a_in_h.ap(), a2a_in_h.ap(), sst, wait=[cc_x])
            pb.dma("sync", dbg_a2a_out_h.ap(), a2a_out_h.ap(), sst, wait=[cc_x])
            pb.dma("sync", dbg_ag_in_h.ap(), ag_in_h.ap(), sst, wait=[cc_x])
            pb.dma("sync", dbg_ag_out_h.ap(), ag_out_h.ap(), sst, wait=[cc_x])

        def finish_early():
            pb.wait("sync", (sst, sst.v), (scc, scc.v), (svv, svv.v),
                    (sss, sss.v), (smm, smm.v), (sev, sev.v))
            for bb in range(B):
                for sub in range(4):
                    pb.dma("sync", out_h.ap()[bb, sub * P:(sub + 1) * P, :],
                           fv(A_X[0], D), sst)
            pb.wait("sync", sst.now)
            block = es.enter_context(nc.Block())
            pb.replay_into(block)

        if MAXPH < 1:
            finish_early()
            return nc

        B_X32 = PH
        B_XIN = PH + 2048
        B_DT = PH + 6144
        B_U = PH + 10240
        B_W = PH + 14336
        B_BB = PH + 18432
        B_BB2 = [PH + 18432, PH + 2048]
        B_A = [PH + 22528, PH + 26624]
        B_H = [PH + 30720, PH + 32768]
        B_YB = PH + 34816

        pstf = [pst[i][:].bitcast(F32)[:, 0:512] for i in range(2)]
        ybank = [psum[i][:] for i in range(6)] + pstf

        x32bf = bv(B_X32, 4096)[0:32, :]
        yev_done = {}
        ysts = {}
        scanS = {}
        ymmS = {}
        convV = {}
        dtmmT = {}
        dtevG = {}
        dbgst = {}
        bd = 0
        for b_i in range(B):
            w = [cc_x]
            if b_i > 0:
                w.append((smm, dtmmT[b_i - 1]))
            pb.emit("gpsimd", lambda E, b_i=b_i: E.dma_start(
                out=x32bf,
                in_=bass.AP(ag_out_h.ap().tensor, b_i * 512,
                            [[2048, 32], [32 * 2048, 8], [1, 512]])),
                wait=w, inc=ldg, incn=16)
            x32rdy = ldg.now
            for dtile in range(2):
                if DIAG_BITS & 2:
                    pb.barrier(sbar)
                wx = [cc_x]
                if bd > 0:
                    wx.append((svv, convV[bd - 1]))
                if dbg and bd == 1:
                    wx.append((sst, dbgst.get("xin", 0)))
                pb.emit("sync", lambda E, b_i=b_i, dtile=dtile: E.dma_start(
                    out=fv(B_XIN, 4096),
                    in_=bass.AP(a2a_out_h.ap().tensor,
                                dtile * P * TPC + b_i * 512,
                                [[TPC, P], [DSL * TPC, 8], [1, 512]])),
                    wait=wx, inc=ld, incn=16)
                xinrdy = ld.now
                for ch in range(8):
                    bank = ch % 2
                    w = [x32rdy]
                    if bd > 0 and ch < 2:
                        w.append((sev, yev_done[bd - 1]))
                    if ch >= 2:
                        w.append((sss, dtevG[(bd, ch - 2)]))
                    elif bd > 0:
                        w.append((sss, dtevG[(bd - 1, 6 + ch)]))
                    pb.emit("tensor", lambda E, ch=ch, dtile=dtile, bank=bank:
                            E.matmul(psum[bank][:],
                                     wdtT_bf[:, dtile * P:(dtile + 1) * P],
                                     x32bf[:, ch * 512:(ch + 1) * 512],
                                     start=True, stop=True), wait=w, inc=smm)
                    w2 = [smm.now]
                    if bd > 0 and ch == 0:
                        w2.append((sev, scanS[(bd - 1, 15)]))
                        if dbg and bd == 1:
                            w2.append((sst, dbgst.get("all", 0)))
                    pb.emit("scalar", lambda E, ch=ch, dtile=dtile, bank=bank:
                            E.activation(out=fv(B_BB + ch * 512, 512),
                                         in_=psum[bank][:], func=AF.Exp,
                                         bias=scanw[:, 74 + dtile:75 + dtile],
                                         scale=1.0),
                            wait=w2, inc=sss, dr=True)
                    pb.emit("scalar", lambda E, ch=ch: E.activation(
                        out=fv(B_DT + ch * 512, 512), in_=fv(B_BB + ch * 512, 512),
                        func=AF.Ln, bias=fc(1.0)), inc=sss, dr=True)
                    dtevG[(bd, ch)] = sss.v
                dtmmT[b_i] = smm.v
                dtrdy = sss.now
                cw_aps = [scanw[:, 64 + dtile * 4 + k:65 + dtile * 4 + k]
                          for k in range(4)]
                pb.emit("vector", lambda E, c=cw_aps[3]: E.tensor_scalar(
                    out=fv(B_U, 4096), in0=fv(B_XIN, 4096), scalar1=c,
                    scalar2=None, op0=ALU.mult), wait=[xinrdy], inc=svv, dr=True)
                for lag in range(1, 4):
                    pb.emit("vector", lambda E, lag=lag, c=cw_aps[3 - lag]:
                            E.scalar_tensor_tensor(
                                out=fv(B_U + lag, 4096 - lag),
                                in0=fv(B_XIN, 4096 - lag), scalar=c,
                                in1=fv(B_U + lag, 4096 - lag),
                                op0=ALU.mult, op1=ALU.add), inc=svv, dr=True)
                convV[bd] = svv.v
                vconv = svv.now
                pb.emit("scalar", lambda E, dtile=dtile: E.activation(
                    out=fv(B_U, 4096), in_=fv(B_U, 4096), func=AF.Silu,
                    bias=scanw[:, 72 + dtile:73 + dtile]),
                    wait=[vconv, dtrdy], inc=sss, dr=True)
                urdy = sss.now
                pb.emit("vector", lambda E: E.tensor_mul(
                    out=fv(B_W, 4096), in0=fv(B_U, 4096), in1=fv(B_DT, 4096)),
                    wait=[urdy], inc=svv)
                wrdy = svv.now
                for n in range(16):
                    par = n % 2
                    col = dtile * 16 + n
                    w = [dtrdy]
                    if n >= 2:
                        w.append((sev, scanS[(bd, n - 2)]))
                    elif bd > 0:
                        w.append((sev, scanS[(bd - 1, 14 + n)]))
                    pb.emit("scalar", lambda E, par=par, col=col: E.activation(
                        out=fv(B_A[par], 4096), in_=fv(B_DT, 4096), func=AF.Exp,
                        scale=scanw[:, col:col + 1]), wait=w, inc=sss)
                    sa_n = sss.now
                    w = [sa_n]
                    if n >= 2:
                        w.append((smm, ymmS[(bd, n - 2)]))
                    elif bd > 0:
                        w.append((smm, ymmS[(bd - 1, 14 + n)]))
                    if n == 0:
                        w.append(wrdy)
                    pb.emit("vector", lambda E, par=par: E.tensor_tensor_scan(
                        out=bv(B_H[par], 4096), data0=fv(B_A[par], 4096),
                        data1=fv(B_W, 4096), initial=0.0,
                        op0=ALU.mult, op1=ALU.add), wait=w, inc=sev)
                    scanS[(bd, n)] = sev.v
                    w = [sev.now]
                    if n == 0 and bd > 0:
                        w.append((sev, yev_done[bd - 1]))

                    dcol = dtile * 16 + n

                    def ymm(E, par=par, n=n, dcol=dcol):
                        for ch in range(8):
                            ins = E.matmul(ybank[ch],
                                           bv(BCD + dcol * 64, 128),
                                           bv(B_H[par] + ch * 256, 512),
                                           start=(n == 0), stop=(n == 15))
                        return ins
                    if bd == 0 and n == 0:
                        w.append(bcdiag_rdy)
                    pb.emit("tensor", ymm, wait=w, inc=smm)
                    ymmS[(bd, n)] = smm.v
                if dbg and bd == 0:
                    pb.dma("sync", dbg_b_h.ap()[0], fv(B_DT, 4096), sst,
                           wait=[dtrdy])
                    pb.dma("sync", dbg_b_h.ap()[1], fv(B_U, 4096), sst,
                           wait=[wrdy])
                    pb.dma("sync", dbg_b_h.ap()[2], fv(B_W, 4096), sst)
                    pb.dma("sync", dbg_b_h.ap()[3], fv(B_XIN, 4096), sst)
                    dbgst["xin"] = sst.v
                    pb.emit("vector", lambda E: E.tensor_copy(
                        out=fv(B_BB, 4096), in_=bv(B_H[1], 4096)),
                        wait=[(sev, scanS[(0, 15)])], inc=sev, dr=True)
                    pb.dma("sync", dbg_b_h.ap()[4], fv(B_BB, 4096), sst,
                           wait=[sev.now])
                    dbgst["all"] = sst.v
                ymm_done = smm.now
                for ch in range(8):
                    w = [ymm_done]
                    if bd > 0 and ch == 0:
                        w.append((sst, ysts[bd - 1]))
                    pb.emit("vector", lambda E, ch=ch: E.tensor_copy(
                        out=bv(B_YB + ch * 256, 512), in_=ybank[ch]),
                        wait=w, inc=sev)
                    pb.emit("sync", lambda E, ch=ch, b_i=b_i, dtile=dtile:
                            E.dma_start(
                                out=a2ay_in_h.ap()[ch, dtile * P:(dtile + 1) * P,
                                                   b_i * 512:(b_i + 1) * 512],
                                in_=bv(B_YB + ch * 256, 512)),
                            wait=[sev.now], inc=sst, incn=16)
                yev_done[bd] = sev.v
                ysts[bd] = sst.v
                bd += 1

        pb.wait("gpsimd", (sst, ysts[7]))
        def cc_y(E):
            with allow_ici_all_to_all():
                return E.collective_compute(
                    "AllToAll", ALU.bypass, replica_groups=[list(range(NC))],
                    ins=[a2ay_in_h.ap().opt()], outs=[a2ay_out_h.ap().opt()])
        pb.emit("gpsimd", cc_y, inc=scc)
        cc_y_done = scc.now
        if dbg:
            pb.dma("sync", dbg_a2ay_out_h.ap(), a2ay_out_h.ap(), sst,
                   wait=[cc_y_done])

        pb.barrier(sbar)
        if MAXPH < 2:
            finish_early()
            return nc

        C_WG = [PH, PH + 6144]
        C_SG = PH + 12288
        C_HT = [PH + 12800, PH + 13056]
        slabldg = {}
        slabmm = {}
        hev = {}
        hst = {}
        it = 0
        for ms in range(4):
            if DIAG_BITS & 4:
                pb.barrier(sbar)
            par = ms % 2
            wq = C_WG[par]
            w = []
            if ms >= 2:
                w.append((smm, slabmm[ms - 2]))
            for kb in range(8):
                pb.emit("gpsimd", lambda E, kb=kb, ms=ms, wq=wq: E.dma_start(
                    out=bv(wq + kb * 384, 768),
                    in_=wgT_h.ap()[kb * P:(kb + 1) * P, ms * 768:(ms + 1) * 768]),
                    wait=(w if kb == 0 else ()), inc=ldg, incn=16)
                pb.emit("gpsimd", lambda E, kb=kb, ms=ms, wq=wq: E.dma_start(
                    out=bv(wq + 3072 + kb * 384, 768),
                    in_=wuT_h.ap()[kb * P:(kb + 1) * P, ms * 768:(ms + 1) * 768]),
                    inc=ldg, incn=16)
            slabrdy = ldg.now
            for q in range(4):
                for mi in range(6):
                    m = ms * 6 + mi
                    bg, bu = 2 + 2 * (it % 2), 3 + 2 * (it % 2)
                    w = [slabrdy]
                    if it >= 2:
                        w.append((sev, hev[it - 2]))

                    def ffmm(E, q=q, mi=mi, wq=wq, bg=bg, bu=bu):
                        for kb in range(8):
                            E.matmul(psum[bg][:],
                                     bv(wq + kb * 384 + mi * 64, 128),
                                     ARb[:, kb * TPC + q * 512:
                                         kb * TPC + (q + 1) * 512],
                                     start=(kb == 0), stop=(kb == 7))
                        for kb in range(8):
                            ins = E.matmul(psum[bu][:],
                                           bv(wq + 3072 + kb * 384 + mi * 64, 128),
                                           ARb[:, kb * TPC + q * 512:
                                               kb * TPC + (q + 1) * 512],
                                           start=(kb == 0), stop=(kb == 7))
                        return ins
                    pb.emit("tensor", ffmm, wait=w, inc=smm)
                    slabmm[ms] = smm.v
                    pb.emit("scalar", lambda E, bg=bg: E.activation(
                        out=fv(C_SG, 512), in_=psum[bg][:], func=AF.Silu,
                        scale=INV127), wait=[smm.now], inc=sss, dr=True)
                    hbuf = C_HT[it % 2]
                    w2 = [sss.now]
                    if it >= 2:
                        w2.append((sst, hst[it - 2]))
                    pb.emit("vector", lambda E, bu=bu, hbuf=hbuf:
                            E.scalar_tensor_tensor(
                                out=bv(hbuf, 512), in0=psum[bu][:],
                                scalar=INV127, in1=fv(C_SG, 512),
                                op0=ALU.mult, op1=ALU.mult),
                            wait=w2, inc=sev)
                    hev[it] = sev.v
                    pb.emit("sync", lambda E, m=m, q=q, hbuf=hbuf: E.dma_start(
                        out=h_spill_h.ap()[m * P:(m + 1) * P,
                                           q * 512:(q + 1) * 512],
                        in_=bv(hbuf, 512)), wait=[sev.now], inc=sst, incn=16)
                    hst[it] = sst.v
                    it += 1
        pb.barrier(sbar)
        if MAXPH < 3:
            finish_early()
            return nc

        gevt = {}
        C_HQ = PH
        C_WDT = PH + 6144
        GSLAB = 38912
        for k in range(24):
            pb.emit("gpsimd", lambda E, k=k: E.dma_start(
                out=bv(C_WDT + k * 512, 1024),
                in_=wdT_h.ap()[k * P:(k + 1) * P, :]), inc=ldg, incn=16)
        wdtrdy = ldg.now
        gev = {}
        for q in range(4):
            if DIAG_BITS & 4:
                pb.barrier(sbar)
            for k in range(24):
                pb.emit("sync", lambda E, q=q, k=k: E.dma_start(
                    out=bv(C_HQ + k * 256, 512),
                    in_=h_spill_h.ap()[k * P:(k + 1) * P, q * 512:(q + 1) * 512]),
                    wait=([(sss, gev[q - 1])] if (q > 0 and k == 0) else ()),
                    inc=ld, incn=16)
            hqrdy = ld.now
            for ti in range(4):
                tt = q * 4 + ti
                b0, b1 = 2 * (ti % 2), 2 * (ti % 2) + 1

                def dmm(E, ti=ti, b0=b0, b1=b1):
                    for k in range(24):
                        E.matmul(psum[b0][:], bv(C_HQ + k * 256 + ti * 64, 128),
                                 bv(C_WDT + k * 512, 512),
                                 start=(k == 0), stop=False)
                        E.matmul(psum[b1][:], bv(C_HQ + k * 256 + ti * 64, 128),
                                 bv(C_WDT + k * 512 + 256, 512),
                                 start=(k == 0), stop=False)
                    E.matmul(psum[b0][:], bv(C_HQ, 0)[0:1, 0:0]
                             if False else psum[b0][:, 0:0], psum[b0][:, 0:0],
                             start=False, stop=True) if False else None
                    return E.nop()
                def dmm2(E, ti=ti, b0=b0, b1=b1):
                    for k in range(24):
                        E.matmul(psum[b0][:], bv(C_HQ + k * 256 + ti * 64, 128),
                                 bv(C_WDT + k * 512, 512),
                                 start=(k == 0), stop=(k == 23))
                        ins = E.matmul(psum[b1][:],
                                       bv(C_HQ + k * 256 + ti * 64, 128),
                                       bv(C_WDT + k * 512 + 256, 512),
                                       start=(k == 0), stop=(k == 23))
                    return ins
                w = [hqrdy, wdtrdy]
                if tt >= 2:
                    w.append((sss, gevt[tt - 2]))
                pb.emit("tensor", dmm2, wait=w, inc=smm)
                pb.emit("scalar", lambda E, tt=tt, b0=b0: E.activation(
                    out=ARb[:, 2 * GSLAB + tt * 1024:2 * GSLAB + tt * 1024 + 512],
                    in_=psum[b0][:], func=AF.Silu, scale=INV127),
                    wait=[smm.now], inc=sss)
                pb.emit("scalar", lambda E, tt=tt, b1=b1: E.activation(
                    out=ARb[:, 2 * GSLAB + tt * 1024 + 512:
                            2 * GSLAB + (tt + 1) * 1024],
                    in_=psum[b1][:], func=AF.Silu, scale=INV127), inc=sss)
                gevt[tt] = sss.v
            gev[q] = sss.v
        if dbg:
            pb.dma("sync", dbg_g_h.ap(), ARb[:, 2 * GSLAB:2 * GSLAB + 16384],
                   sst, wait=[sss.now])

        pb.barrier(sbar)
        if MAXPH < 4:
            finish_early()
            return nc

        tp2_last = {}
        tp2_ev = {}
        xfree = {}
        wommL = {}
        xnsp = {}
        C_PROD = PH
        C_WOT = PH + 4096
        C_X = PH + 12288
        C_XN = PH + 13312
        C_S = PH + 14336
        C_GF = PH + 15360
        C_X1 = PH + 16384
        C_XN2F = PH + 17408
        C_JK = PH + 18432
        C_SZ = PH + 19456
        C_XNBF2 = PH + 19712
        XN2TOK = PH + 20480
        for k in range(16):
            pb.emit("gpsimd", lambda E, k=k: E.dma_start(
                out=bv(C_WOT + k * 512, 1024),
                in_=woT_h.ap()[k * P:(k + 1) * P, :]), inc=ldg, incn=16)
        wotrdy = ldg.now
        prodv = {}
        x1st = {}
        tpc1 = {}
        for tg in range(4):
            for k in range(16):
                sblk, rb = k // 2, (k % 2) * P
                w = [cc_y_done]
                if tg > 0 and k == 0:
                    w.append((svv, prodv[tg - 1]))
                    w.append((smm, wommL[tg - 1]))
                pb.emit("sync", lambda E, k=k, sblk=sblk, rb=rb, tg=tg:
                        E.dma_start(
                            out=bv(C_PROD + k * 256, 512),
                            in_=a2ay_in_h.ap()[0, 0, 0:512] if False else
                            a2ay_out_h.ap()[sblk, rb:rb + P,
                                            tg * 512:(tg + 1) * 512]),
                        wait=w, inc=ld, incn=16)
                wsz = [(svv, svv.v)] if k > 0 else ([] if tg == 0 else
                                                    [(svv, prodv[tg - 1])])
                pb.emit("sync", lambda E, k=k, tg=tg: E.dma_start(
                    out=bv(C_SZ, 512),
                    in_=sz_spill_h.ap()[k * P:(k + 1) * P,
                                        tg * 512:(tg + 1) * 512]),
                    wait=wsz, inc=ld, incn=16)
                pb.emit("vector", lambda E, k=k: E.tensor_mul(
                    out=bv(C_PROD + k * 256, 512), in0=bv(C_PROD + k * 256, 512),
                    in1=bv(C_SZ, 512)), wait=[ld.now], inc=svv, dr=True)
            prodv[tg] = svv.v
            prodrdy = svv.now
            if dbg and tg == 0:
                pb.dma("sync", dbg_p_h.ap(), ARb[:, 2 * C_PROD:2 * C_PROD + 8192],
                       sst, wait=[prodrdy])
            for ti in range(4):
                tt = tg * 4 + ti
                b0, b1 = 2 * (ti % 2), 2 * (ti % 2) + 1
                if DIAG_BITS & 1:
                    pb.barrier(sbar)

                def womm(E, ti=ti, b0=b0, b1=b1):
                    for k in range(16):
                        E.matmul(psum[b0][:], bv(C_PROD + k * 256 + ti * 64, 128),
                                 bv(C_WOT + k * 512, 512),
                                 start=(k == 0), stop=False)
                        E.matmul(psum[b1][:], bv(C_PROD + k * 256 + ti * 64, 128),
                                 bv(C_WOT + k * 512 + 256, 512),
                                 start=(k == 0), stop=False)
                    E.matmul(psum[b0][:], ones_bf[:], bo_bf[:, 0:512],
                             start=False, stop=True)
                    return E.matmul(psum[b1][:], ones_bf[:], bo_bf[:, 512:1024],
                                    start=False, stop=True)
                w = [prodrdy, wotrdy]
                if tt >= 2:
                    w.append((svv, tpc1[tt - 2]))
                pb.emit("tensor", womm, wait=w, inc=smm)
                wommL[tg] = smm.v
                womm_d = smm.now
                bq, sub = tt // 4, tt % 4
                pb.emit("sync", lambda E, bq=bq, sub=sub: E.dma_start(
                    out=fv(C_X, D), in_=x_h.ap()[bq, sub * P:(sub + 1) * P, :]),
                    wait=([(svv, xfree[tt - 1])] if tt >= 1 else ()),
                    inc=ld, incn=16)

                def bcast0(col, n=D):
                    ap = stats0[:, col:col + 1]
                    return bass.AP(ap.tensor, ap.offset, [list(ap.ap[0]), [0, n]])
                pb.emit("vector", lambda E, tt=tt: E.tensor_sub(
                    out=fv(C_XN, D), in0=fv(C_X, D), in1=bcast0(2 * tt)),
                    wait=[ld.now], inc=svv, dr=True)
                pb.emit("vector", lambda E, tt=tt: E.tensor_mul(
                    out=fv(C_XN, D), in0=fv(C_XN, D), in1=bcast0(2 * tt + 1)),
                    inc=svv, dr=True)
                pb.emit("vector", lambda E, b0=b0: E.scalar_tensor_tensor(
                    out=fv(C_S, 512), in0=fv(C_XN, 512), scalar=dvbar[:, 0:1],
                    in1=psum[b0][:], op0=ALU.mult, op1=ALU.add),
                    wait=[womm_d], inc=svv)
                pb.emit("vector", lambda E, b1=b1: E.scalar_tensor_tensor(
                    out=fv(C_S + 512, 512), in0=fv(C_XN + 512, 512),
                    scalar=dvbar[:, 0:1], in1=psum[b1][:],
                    op0=ALU.mult, op1=ALU.add), inc=svv, dr=True)
                if dbg and tt == 1:
                    pb.emit("vector", lambda E, b0=b0: E.tensor_copy(
                        out=fv(C_JK, 512), in_=psum[b0][:]), inc=svv, dr=True)
                    pb.emit("vector", lambda E, b1=b1: E.tensor_copy(
                        out=fv(C_JK + 512, 512), in_=psum[b1][:]),
                        inc=svv, dr=True)
                    pb.dma("sync", dbg_c_h.ap()[2], fv(C_JK, D), sst,
                           wait=[svv.now])
                    pb.dma("sync", dbg_c_h.ap()[1], fv(C_S, D), sst)
                tpc1[tt] = svv.v
                pb.emit("vector", lambda E, tt=tt: E.tensor_copy(
                    out=fv(C_GF, D),
                    in_=ARb[:, 2 * GSLAB + tt * 1024:2 * GSLAB + (tt + 1) * 1024]),
                    inc=svv, dr=True)
                pb.emit("vector", lambda E: E.tensor_mul(
                    out=fv(C_S, D), in0=fv(C_S, D), in1=fv(C_GF, D)),
                    inc=svv, dr=True)
                pb.emit("vector", lambda E: E.tensor_add(
                    out=fv(C_X1, D), in0=fv(C_X, D), in1=fv(C_S, D)),
                    inc=svv, dr=True)
                xfree[tt] = svv.v
                x1rdy = svv.now
                if dbg and tt == 1:
                    pb.dma("sync", dbg_c_h.ap()[0], fv(C_XN, D), sst,
                           wait=[x1rdy])
                    pb.dma("sync", dbg_c_h.ap()[3], fv(C_X1, D), sst)
                pb.emit("sync", lambda E, tt=tt: E.dma_start(
                    out=x1_spill_h.ap()[tt], in_=fv(C_X1, D)),
                    wait=[x1rdy], inc=sst, incn=16)
                x1st[tt] = sst.v
                pb.emit("vector", lambda E: E.reduce_sum(
                    out=tiny[:, 34:35], in_=fv(C_X1, D), axis=AX.X),
                    inc=svv, dr=True)
                pb.emit("vector", lambda E: E.scalar_tensor_tensor(
                    out=fv(C_JK, D), in0=fv(C_X1, D), scalar=1.0,
                    in1=fv(C_X1, D), op0=ALU.bypass, op1=ALU.mult),
                    inc=svv, dr=True)
                pb.emit("vector", lambda E: E.reduce_sum(
                    out=tiny[:, 35:36], in_=fv(C_JK, D), axis=AX.X),
                    inc=svv, dr=True)
                pb.emit("vector", lambda E, tt=tt: E.tensor_scalar(
                    out=stats1[:, 4 * tt:4 * tt + 1], in0=tiny[:, 34:35],
                    scalar1=1.0 / D, scalar2=None, op0=ALU.mult),
                    inc=svv, dr=True)
                pb.emit("vector", lambda E: E.tensor_scalar(
                    out=tiny[:, 37:38], in0=tiny[:, 35:36], scalar1=1.0 / D,
                    scalar2=None, op0=ALU.mult), inc=svv, dr=True)
                pb.emit("vector", lambda E, tt=tt: E.tensor_mul(
                    out=tiny[:, 38:39], in0=stats1[:, 4 * tt:4 * tt + 1],
                    in1=stats1[:, 4 * tt:4 * tt + 1]), inc=svv, dr=True)
                pb.emit("vector", lambda E: E.tensor_sub(
                    out=tiny[:, 38:39], in0=tiny[:, 37:38], in1=tiny[:, 38:39]),
                    inc=svv, dr=True)
                vvar2 = svv.now
                pb.emit("scalar", lambda E: E.activation(
                    out=tiny[:, 39:40], in_=tiny[:, 38:39], func=AF.Ln,
                    bias=fc(EPS)), wait=[vvar2], inc=sss, dr=True)
                pb.emit("scalar", lambda E, tt=tt: E.activation(
                    out=stats1[:, 4 * tt + 1:4 * tt + 2], in_=tiny[:, 39:40],
                    func=AF.Exp, scale=-0.5), inc=sss, dr=True)
                rstd2rdy = sss.now

                def bcast2(col, n=D):
                    ap = stats1[:, col:col + 1]
                    return bass.AP(ap.tensor, ap.offset, [list(ap.ap[0]), [0, n]])
                pb.emit("vector", lambda E, tt=tt: E.tensor_sub(
                    out=fv(C_XN2F, D), in0=fv(C_X1, D), in1=bcast2(4 * tt)),
                    wait=[rstd2rdy], inc=svv, dr=True)
                pb.emit("vector", lambda E, tt=tt: E.tensor_mul(
                    out=fv(C_XN2F, D), in0=fv(C_XN2F, D),
                    in1=bcast2(4 * tt + 1)), inc=svv, dr=True)

                pb.emit("vector", lambda E: E.reduce_sum(
                    out=tiny[:, 34:35], in_=fv(C_XN2F, D), axis=AX.X),
                    inc=svv, dr=True)
                pb.emit("vector", lambda E: E.scalar_tensor_tensor(
                    out=fv(C_JK, D), in0=fv(C_XN2F, D), scalar=1.0,
                    in1=fv(C_XN2F, D), op0=ALU.bypass, op1=ALU.mult),
                    inc=svv, dr=True)
                pb.emit("vector", lambda E: E.reduce_sum(
                    out=tiny[:, 35:36], in_=fv(C_JK, D), axis=AX.X),
                    inc=svv, dr=True)
                pb.emit("vector", lambda E: E.tensor_scalar(
                    out=tiny[:, 36:37], in0=tiny[:, 34:35], scalar1=1.0 / D,
                    scalar2=None, op0=ALU.mult), inc=svv, dr=True)
                pb.emit("vector", lambda E: E.tensor_scalar(
                    out=tiny[:, 37:38], in0=tiny[:, 35:36], scalar1=1.0 / D,
                    scalar2=None, op0=ALU.mult), inc=svv, dr=True)
                pb.emit("vector", lambda E: E.tensor_mul(
                    out=tiny[:, 38:39], in0=tiny[:, 36:37], in1=tiny[:, 36:37]),
                    inc=svv, dr=True)
                pb.emit("vector", lambda E: E.tensor_sub(
                    out=tiny[:, 38:39], in0=tiny[:, 37:38], in1=tiny[:, 38:39]),
                    inc=svv, dr=True)
                vvar3 = svv.now
                pb.emit("scalar", lambda E: E.activation(
                    out=tiny[:, 39:40], in_=tiny[:, 38:39], func=AF.Ln,
                    bias=fc(EPS)), wait=[vvar3], inc=sss, dr=True)
                pb.emit("scalar", lambda E: E.activation(
                    out=tiny[:, 40:41], in_=tiny[:, 39:40],
                    func=AF.Exp, scale=-0.5), inc=sss, dr=True)
                r3rdy = sss.now
                pb.emit("vector", lambda E, tt=tt: E.tensor_mul(
                    out=stats1[:, 4 * tt + 2:4 * tt + 3],
                    in0=stats1[:, 4 * tt + 1:4 * tt + 2], in1=tiny[:, 40:41]),
                    wait=[r3rdy], inc=svv, dr=True)
                pb.emit("vector", lambda E, tt=tt: E.tensor_mul(
                    out=tiny[:, 41:42], in0=stats1[:, 4 * tt:4 * tt + 1],
                    in1=stats1[:, 4 * tt + 1:4 * tt + 2]), inc=svv, dr=True)
                pb.emit("vector", lambda E: E.tensor_add(
                    out=tiny[:, 41:42], in0=tiny[:, 41:42], in1=tiny[:, 36:37]),
                    inc=svv, dr=True)
                pb.emit("vector", lambda E: E.tensor_mul(
                    out=tiny[:, 41:42], in0=tiny[:, 41:42], in1=tiny[:, 40:41]),
                    inc=svv, dr=True)
                pb.emit("vector", lambda E, tt=tt: E.tensor_scalar(
                    out=stats1[:, 4 * tt + 3:4 * tt + 4], in0=tiny[:, 41:42],
                    scalar1=-1.0, scalar2=None, op0=ALU.mult), inc=svv, dr=True)
                xnbf2 = bv(C_XNBF2, D)
                wxb = [(smm, tp2_last[tt - 1])] if tt >= 1 else []
                if tt >= 1:
                    wxb.append((sst, xnsp[tt - 1]))
                pb.emit("vector", lambda E, xnbf2=xnbf2: E.tensor_copy(
                    out=xnbf2, in_=fv(C_XN2F, D)),
                    wait=wxb, inc=svv, dr=True)
                vxn2 = svv.now
                pb.emit("sync", lambda E, tt=tt, xnbf2=xnbf2: E.dma_start(
                    out=xn2sp_h.ap()[tt], in_=xnbf2), wait=[vxn2],
                    inc=sst, incn=16)
                xnsp[tt] = sst.v
                for dblk in range(8):
                    bank = dblk % 2
                    idx = tt * 8 + dblk
                    w = [vxn2]
                    if idx >= 2:
                        w.append((sev, tp2_ev[idx - 2]))
                    pb.emit("tensor", lambda E, bank=bank, dblk=dblk, xnbf2=xnbf2:
                            E.transpose(
                        pst[bank][:, 0:P], xnbf2[:, dblk * P:(dblk + 1) * P],
                        ident_bf[:]), wait=w, inc=smm)
                    tp2_last[tt] = smm.v
                    pb.emit("vector", lambda E, bank=bank, dblk=dblk, tt=tt:
                            E.tensor_copy(
                                out=ARb[:, dblk * TPC + tt * P:
                                        dblk * TPC + (tt + 1) * P],
                                in_=pst[bank][:, 0:P]), wait=[smm.now], inc=sev)
                    tp2_ev[idx] = sev.v


        pb.barrier(sbar)
        if MAXPH < 6:
            finish_early()
            return nc

        L_XLO = PH
        L_W1 = PH + 8192
        L_W1L = PH + 10240
        L_H1F = PH + 12288
        L_W2F = PH + 20480
        L_SC = PH + 20512
        L_MK = PH + 20640
        L_W1S = PH + 20768
        L_W1T = PH + 21280
        L_EW = [PH, PH + 4096]
        GS2 = 2 * GSLAB

        ebstage = sbt("ebstage", [1, D], BF16)
        pb.wait("sync", (sbar, sbar.v))
        w1lo = {}
        for kb in range(8):
            pb.dma("sync", fv(L_W1S, 512), w1_h.ap()[kb * P:(kb + 1) * P, :],
                   ld, wait=([(svv, w1lo[kb - 1])] if kb else ()))
            pb.emit("vector", lambda E, kb=kb: E.tensor_copy(
                out=bv(L_W1 + kb * 256, 512), in_=fv(L_W1S, 512)),
                wait=[ld.now], inc=svv, dr=True)
            pb.emit("vector", lambda E, kb=kb: E.tensor_copy(
                out=fv(L_W1T, 512), in_=bv(L_W1 + kb * 256, 512)),
                inc=svv, dr=True)
            pb.emit("vector", lambda E: E.tensor_sub(
                out=fv(L_W1S, 512), in0=fv(L_W1S, 512), in1=fv(L_W1T, 512)),
                inc=svv, dr=True)
            pb.emit("vector", lambda E, kb=kb: E.tensor_copy(
                out=bv(L_W1L + kb * 256, 512), in_=fv(L_W1S, 512)),
                inc=svv, dr=True)
            w1lo[kb] = svv.v
        for kb in range(4):
            pb.dma("sync", fv(L_W2F + kb * 8, 8),
                   w2_h.ap()[kb * P:(kb + 1) * P, :], ld)
        l1wrdy = (svv, svv.v)
        l1wld = ld.now

        XS0, XS1, XS2, XS3 = (L_H1F, L_H1F + 1024, L_H1F + 2048, L_H1F + 3072)
        xlt = {}
        xle = {}
        for tt in range(16):
            w = [(sst, sst.v)]
            if tt >= 1:
                w.append((svv, xlt[tt - 1]))
            pb.dma("sync", fv(XS0, D), x1_spill_h.ap()[tt], ld, wait=w)

            def b2c(col, n=D):
                ap = stats1[:, col:col + 1]
                return bass.AP(ap.tensor, ap.offset, [list(ap.ap[0]), [0, n]])
            pb.emit("vector", lambda E, tt=tt: E.tensor_sub(
                out=fv(XS1, D), in0=fv(XS0, D), in1=b2c(4 * tt)),
                wait=[ld.now], inc=svv, dr=True)
            pb.emit("vector", lambda E, tt=tt: E.tensor_mul(
                out=fv(XS1, D), in0=fv(XS1, D), in1=b2c(4 * tt + 1)),
                inc=svv, dr=True)
            pb.emit("vector", lambda E: E.tensor_copy(
                out=bv(XS2, D), in_=fv(XS1, D)), inc=svv, dr=True)
            pb.emit("vector", lambda E: E.tensor_copy(
                out=fv(XS3, D), in_=bv(XS2, D)), inc=svv, dr=True)
            pb.emit("vector", lambda E: E.tensor_sub(
                out=fv(XS1, D), in0=fv(XS1, D), in1=fv(XS3, D)),
                inc=svv, dr=True)
            pb.emit("vector", lambda E: E.tensor_copy(
                out=bv(XS2, D), in_=fv(XS1, D)), inc=svv, dr=True)
            xlt[tt] = svv.v
            vlo = svv.now
            for dblk in range(8):
                bank = dblk % 2
                pb.emit("tensor", lambda E, bank=bank, dblk=dblk: E.transpose(
                    pst[bank][:, 0:P], bv(XS2 + dblk * 64, 128), ident_bf[:]),
                    wait=[vlo, (sev, max(sev.v - 1, 0))], inc=smm)
                pb.emit("vector", lambda E, bank=bank, dblk=dblk, tt=tt:
                        E.tensor_copy(
                            out=ARb[:, 2 * L_XLO + dblk * TPC + tt * P:
                                    2 * L_XLO + dblk * TPC + (tt + 1) * P],
                            in_=pst[bank][:, 0:P]), wait=[smm.now], inc=sev)
            xle[tt] = sev.v
        xlordy = sev.now

        h1ev = {}
        it = 0
        for m in range(4):
            for tc in range(4):
                bank = it % 4

                def h1mm(E, m=m, tc=tc, bank=bank):
                    for kb in range(8):
                        E.matmul(
                            psum[bank][:], bv(L_W1 + kb * 256 + m * 64, 128),
                            ARb[:, kb * TPC + tc * 512:kb * TPC + (tc + 1) * 512],
                            start=(kb == 0), stop=False)
                    for kb in range(8):
                        E.matmul(
                            psum[bank][:], bv(L_W1L + kb * 256 + m * 64, 128),
                            ARb[:, kb * TPC + tc * 512:kb * TPC + (tc + 1) * 512],
                            start=False, stop=False)
                    for kb in range(8):
                        ins = E.matmul(
                            psum[bank][:], bv(L_W1 + kb * 256 + m * 64, 128),
                            ARb[:, 2 * L_XLO + kb * TPC + tc * 512:
                                2 * L_XLO + kb * TPC + (tc + 1) * 512],
                            start=False, stop=(kb == 7))
                    return ins
                w = [l1wrdy, xlordy]
                if it >= 4:
                    w.append((sss, h1ev[it - 4]))
                pb.emit("tensor", h1mm, wait=w, inc=smm)
                pb.emit("scalar", lambda E, m=m, tc=tc, bank=bank: E.activation(
                    out=fv(L_H1F + m * 2048 + tc * 512, 512), in_=psum[bank][:],
                    func=AF.Relu, bias=b1_sb[:, m:m + 1]),
                    wait=[smm.now], inc=sss)
                h1ev[it] = sss.v
                it += 1
        h1rdy = sss.now

        scev = {}
        for tt in range(16):
            bank = tt % 2

            def scmm(E, tt=tt, bank=bank):
                for mh in range(4):
                    E.matmul(psum[bank][:, 0:8],
                             fv(L_H1F + mh * 2048 + tt * 128, 128),
                             fv(L_W2F + mh * 8, 8),
                             start=(mh == 0), stop=False)
                return E.matmul(psum[bank][:, 0:8], ones_f[:], b2_sb[:],
                                start=False, stop=True)
            w = [h1rdy, l1wld]
            if tt >= 2:
                w.append((svv, scev[tt - 2]))
            pb.emit("tensor", scmm, wait=w, inc=smm)
            pb.emit("vector", lambda E, tt=tt, bank=bank: E.tensor_copy(
                out=fv(L_SC + tt * 8, 8), in_=psum[bank][:, 0:8]),
                wait=[smm.now], inc=svv, dr=True)
            scev[tt] = svv.v
            pb.emit("vector", lambda E, tt=tt: E.reduce_max(
                out=tiny[:, 42:43], in_=fv(L_SC + tt * 8, 8), axis=AX.X),
                inc=svv, dr=True)

            def bct(n=8):
                ap = tiny[:, 42:43]
                return bass.AP(ap.tensor, ap.offset, [list(ap.ap[0]), [0, n]])
            pb.emit("vector", lambda E, tt=tt: E.tensor_tensor(
                out=fv(L_MK + tt * 8, 8), in0=fv(L_SC + tt * 8, 8), in1=bct(),
                op=ALU.is_ge), inc=svv, dr=True)
            pb.emit("vector", lambda E, tt=tt: E.tensor_copy(
                out=tiny[:, 43:44], in_=fv(L_MK + tt * 8, 1)), inc=svv, dr=True)
            for e in range(1, 8):
                pb.emit("vector", lambda E: E.tensor_scalar(
                    out=tiny[:, 44:45], in0=tiny[:, 43:44], scalar1=-1.0,
                    scalar2=1.0, op0=ALU.mult, op1=ALU.add), inc=svv, dr=True)
                pb.emit("vector", lambda E, tt=tt, e=e: E.tensor_mul(
                    out=fv(L_MK + tt * 8 + e, 1), in0=fv(L_MK + tt * 8 + e, 1),
                    in1=tiny[:, 44:45]), inc=svv, dr=True)
                pb.emit("vector", lambda E, tt=tt, e=e: E.tensor_add(
                    out=tiny[:, 43:44], in0=tiny[:, 43:44],
                    in1=fv(L_MK + tt * 8 + e, 1)), inc=svv, dr=True)
        mkrdy = svv.now

        ewld = {}
        exev = {}
        it = 0
        h1mm_done = smm.v
        for e in range(8):
            if DIAG_BITS & 8:
                pb.barrier(sbar)
            buf = L_EW[e % 2]
            w = [(smm, h1mm_done)]
            if e >= 2:
                w.append((smm, ewmm[e - 2]))
            for kb in range(8):
                pb.emit("gpsimd", lambda E, e=e, kb=kb, buf=buf: E.dma_start(
                    out=bv(buf + kb * 512, 1024),
                    in_=eW_h.ap()[e, kb * P:(kb + 1) * P, :]),
                    wait=(w if kb == 0 else ()), inc=ldg, incn=16)
            webias = [(smm, ewmm[e - 1])] if e >= 1 else []
            pb.emit("gpsimd", lambda E, e=e: E.dma_start(
                out=ebstage[:], in_=eb_h.ap()[e]), wait=webias, inc=ldg, incn=16)
            ewld[e] = ldg.now
            if e == 0:
                ewmm = {}
            for tt in range(16):
                b0, b1 = 2 * (it % 2), 2 * (it % 2) + 1

                def exmm(E, e=e, tt=tt, buf=buf, b0=b0, b1=b1):
                    for kb in range(8):
                        E.matmul(psum[b0][:], ARb[:, kb * TPC + tt * P:
                                                  kb * TPC + (tt + 1) * P],
                                 bv(buf + kb * 512, 512),
                                 start=(kb == 0), stop=False)
                        E.matmul(psum[b1][:], ARb[:, kb * TPC + tt * P:
                                                  kb * TPC + (tt + 1) * P],
                                 bv(buf + kb * 512 + 256, 512),
                                 start=(kb == 0), stop=False)
                    E.matmul(psum[b0][:], ones_bf[:], ebstage[:, 0:512],
                             start=False, stop=True)
                    return E.matmul(psum[b1][:], ones_bf[:], ebstage[:, 512:1024],
                                    start=False, stop=True)
                w = [ewld[e], mkrdy]
                if it >= 2:
                    w.append((svv, exev[it - 2]))
                pb.emit("tensor", exmm, wait=w, inc=smm)
                ewmm[e] = smm.v
                mk = fv(L_MK + tt * 8 + e, 1)
                if e == 0:
                    pb.emit("vector", lambda E, tt=tt, b0=b0, mk=mk:
                            E.tensor_scalar(
                                out=ARb[:, GS2 + tt * 1024:GS2 + tt * 1024 + 512],
                                in0=psum[b0][:], scalar1=mk, scalar2=None,
                                op0=ALU.mult), wait=[smm.now], inc=svv)
                    pb.emit("vector", lambda E, tt=tt, b1=b1, mk=mk:
                            E.tensor_scalar(
                                out=ARb[:, GS2 + tt * 1024 + 512:
                                        GS2 + (tt + 1) * 1024],
                                in0=psum[b1][:], scalar1=mk, scalar2=None,
                                op0=ALU.mult), inc=svv, dr=True)
                else:
                    pb.emit("vector", lambda E, tt=tt, b0=b0, mk=mk:
                            E.scalar_tensor_tensor(
                                out=ARb[:, GS2 + tt * 1024:GS2 + tt * 1024 + 512],
                                in0=psum[b0][:], scalar=mk,
                                in1=ARb[:, GS2 + tt * 1024:GS2 + tt * 1024 + 512],
                                op0=ALU.mult, op1=ALU.add),
                            wait=[smm.now], inc=svv)
                    pb.emit("vector", lambda E, tt=tt, b1=b1, mk=mk:
                            E.scalar_tensor_tensor(
                                out=ARb[:, GS2 + tt * 1024 + 512:
                                        GS2 + (tt + 1) * 1024],
                                in0=psum[b1][:], scalar=mk,
                                in1=ARb[:, GS2 + tt * 1024 + 512:
                                         GS2 + (tt + 1) * 1024],
                                op0=ALU.mult, op1=ALU.add), inc=svv, dr=True)
                exev[it] = svv.v
                it += 1
        if dbg:
            pb.dma("sync", dbg_r_h.ap(), ARb[:, GS2:GS2 + 16384], sst,
                   wait=[svv.now])
            pb.dma("sync", dbg_sc_h.ap()[:, 0:128], fv(L_SC, 128), sst)
            pb.dma("sync", dbg_sc_h.ap()[:, 128:256], fv(L_MK, 128), sst)
        g2rdy_marks = []
        for tt in range(16):
            pb.emit("scalar", lambda E, tt=tt: E.activation(
                out=ARb[:, GS2 + tt * 1024:GS2 + (tt + 1) * 1024],
                in_=ARb[:, GS2 + tt * 1024:GS2 + (tt + 1) * 1024],
                func=AF.Silu), wait=[svv.now], inc=sss, dr=True)
        g2rdy = sss.now

        pb.barrier(sbar)
        if MAXPH < 7:
            finish_early()
            return nc

        L_KS = [PH, PH + 2048, PH + 4096]
        L_QMK = [PH + 6144, PH + 8192, PH + 10240]
        L_SMQ = PH + 12288
        L_SMK = PH + 14336
        L_YSM = PH + 16384
        L_XD = PH + 18432
        L_XTOK = PH + 20480
        L_T1 = PH + 22528
        L_T2 = PH + 23552
        L_T3 = PH + 24576
        L_XJ = L_T3
        L_RSF = PH + 25600
        L_BJ = PH + 26112
        L_WBDY = PH + 26368
        L_WBDQ = PH + 26880
        SCL = 36864
        A_D = SCL
        TT_D = SCL + 192
        C_D = SCL + 1728
        WSC = SCL + 1920
        wkbd_sb = sbt("wkbd_sb", [P, 8 * P], BF16)
        wbbd_sb = sbt("wbbd_sb", [P, 8 * 8], BF16)
        brep2t = sbt("brep2t", [P, 2], BF16)

        pb.dma("gpsimd", brep2t[:], brep2t_h.ap(), ldg)
        for hp in range(8):
            pb.emit("gpsimd", lambda E, hp=hp: E.dma_start(
                out=bv(L_WBDY + hp * 64, 128), in_=wy_bd_h.ap()[hp]),
                inc=ldg, incn=16)
            pb.emit("gpsimd", lambda E, hp=hp: E.dma_start(
                out=bv(L_WBDQ + hp * 64, 128), in_=wq_bd_h.ap()[hp]),
                inc=ldg, incn=16)
            pb.emit("gpsimd", lambda E, hp=hp: E.dma_start(
                out=wkbd_sb[:, hp * P:(hp + 1) * P], in_=wk_bd_h.ap()[hp]),
                inc=ldg, incn=16)
            pb.emit("gpsimd", lambda E, hp=hp: E.dma_start(
                out=wbbd_sb[:, hp * 8:(hp + 1) * 8], in_=wb_bd_h.ap()[hp]),
                inc=ldg, incn=16)
        wbdrdy = ldg.now

        def scl3(base_col):
            return bass.AP(AR[:].tensor, base_col, [[ARN, P], [16, 4], [1, 16]])

        def bj3(c):
            return bass.AP(AR[:].tensor, L_BJ + c, [[ARN, P], [64, 4], [4, 16]])

        def wbc16(base_col, jt):
            return bass.AP(AR[:].tensor, base_col + jt * 16,
                           [[ARN, P], [1, 16], [0, 64]])

        def hview(off_f32, n=1024):
            return fv(off_f32, n).rearrange("p (h d) -> p h d", h=16)

        def bview_h(off_f32_b16, jt):
            return bv(off_f32_b16 + jt * 512, 1024).rearrange(
                "p (h d) -> p h d", h=16)

        fam_lhs = {
            "y": lambda hp: bv(L_WBDY + hp * 64, 128),
            "q": lambda hp: bv(L_WBDQ + hp * 64, 128),
            "k": lambda hp: wkbd_sb[:, hp * P:(hp + 1) * P],
        }
        fam_ti = {"y": 0, "q": 1, "k": 2}
        fam_dst = {"y": L_YSM, "q": L_SMQ, "k": L_SMK}
        sridx = {(0, 0): 0, (1, 0): 1, (1, 1): 2, (2, 0): 3, (2, 1): 4,
                 (2, 2): 5}

        st_out = {}
        for t in range(4):
            if DIAG_BITS & 16:
                pb.barrier(sbar)
            if t > 0:
                pb.wait("tensor", (svv, st_out[t - 1]))
            for jt in range(4):
                w = [(svv, svv.v), (sev, sev.v)] if jt == 0 else []
                pb.emit("sync", lambda E, t=t, jt=jt: E.dma_start(
                    out=bv(L_XTOK + jt * 512, 1024),
                    in_=xn2sp_h.ap()[t * 4 + jt]), wait=w, inc=ld, incn=16)
            xtokrdy = ld.now
            for s in range(t):
                for jt in range(4):
                    tile = t * 4 + jt
                    pb.emit("vector", lambda E, s=s, jt=jt:
                            E.tensor_mul(
                                out=fv(L_T1, 1024),
                                in0=bv(L_KS[s] + jt * 512, 1024),
                                in1=bv(L_XTOK + jt * 512, 1024)),
                            wait=[xtokrdy], inc=svv, dr=True)
                    pb.emit("vector", lambda E, s=s, jt=jt: E.tensor_reduce(
                        out=fv(A_D + s * 64 + jt * 16, 16),
                        in_=hview(L_T1), axis=AX.X, op=ALU.add),
                        inc=svv, dr=True)
            adots = svv.now

            def emit_composite(fi, t):
                for jt in range(4):
                    tile = t * 4 + jt
                    wxi = [xtokrdy]
                    if jt == 0:
                        wxi.append((smm, smm.v))
                    pb.emit("vector", lambda E, jt=jt: E.tensor_copy(
                        out=fv(L_XJ, 1024),
                        in_=bv(L_XTOK + jt * 512, 1024)),
                        wait=wxi, inc=svv, dr=True)
                    for r in range(t):
                        first = True
                        for s in range(r, t):
                            sl = sridx[(s, r)]
                            if first:
                                pb.emit("vector", lambda E, fi=fi, sl=sl, s=s,
                                        jt=jt: E.tensor_mul(
                                            out=fv(WSC + jt * 16, 16),
                                            in0=fv(TT_D + (fi * 6 + sl) * 64 +
                                                   jt * 16, 16),
                                            in1=fv(A_D + s * 64 + jt * 16, 16)),
                                        inc=svv, dr=True)
                                first = False
                            else:
                                pb.emit("vector", lambda E, fi=fi, sl=sl, s=s,
                                        jt=jt: E.tensor_mul(
                                            out=fv(WSC + 64, 16),
                                            in0=fv(TT_D + (fi * 6 + sl) * 64 +
                                                   jt * 16, 16),
                                            in1=fv(A_D + s * 64 + jt * 16, 16)),
                                        inc=svv, dr=True)
                                pb.emit("vector", lambda E, jt=jt: E.tensor_add(
                                    out=fv(WSC + jt * 16, 16),
                                    in0=fv(WSC + jt * 16, 16),
                                    in1=fv(WSC + 64, 16)), inc=svv, dr=True)
                        pb.emit("vector", lambda E, r=r, jt=jt: E.tensor_mul(
                            out=hview(L_T1), in0=bview_h(L_QMK[r], jt),
                            in1=wbc16(WSC, jt)), inc=svv, dr=True)
                        pb.emit("vector", lambda E: E.tensor_add(
                            out=fv(L_XJ, 1024), in0=fv(L_XJ, 1024),
                            in1=fv(L_T1, 1024)), inc=svv, dr=True)
                    pb.emit("vector", lambda E: E.tensor_copy(
                        out=bv(L_T2, 1024), in_=fv(L_XJ, 1024)),
                        inc=svv, dr=True)
                    vxj = svv.now
                    for c in range(8):
                        bank = c % 2
                        pb.emit("tensor", lambda E, c=c, bank=bank: E.transpose(
                            pst[bank][:, 0:P],
                            bv(L_T2 + c * 64, 128), ident_bf[:]),
                            wait=[vxj, (sev, max(sev.v - 1, 0))],
                            inc=smm)
                        pb.emit("vector", lambda E, c=c, jt=jt, bank=bank:
                                E.tensor_copy(
                                    out=bv(L_XD + c * 256 + jt * 64, 128),
                                    in_=pst[bank][:, 0:P]),
                                wait=[smm.now], inc=sev)

            for fname in ("y", "q", "k"):
                if DIAG_BITS & 32:
                    pb.barrier(sbar)
                fi = fam_ti[fname]
                if t > 0:
                    emit_composite(fi, t)
                xdrdy = sev.now
                expd = {}
                for hp in range(8):
                    bank = hp % 2
                    if t > 0:
                        rhs = bv(L_XD + hp * 512, 512)
                    else:
                        rhs = ARb[:, hp * TPC + t * 512:hp * TPC + (t + 1) * 512]
                    w = [wbdrdy, xdrdy, g2rdy]
                    if hp >= 2:
                        w.append((sss, expd[hp - 2]))
                    pb.emit("tensor", lambda E, fname=fname, hp=hp, bank=bank,
                            rhs=rhs: E.matmul(
                                psum[bank][:], fam_lhs[fname](hp), rhs,
                                start=True, stop=True), wait=w, inc=smm)
                    wexp = [smm.now]
                    if hp < 2:
                        wexp.append((svv, svv.v))
                    pb.emit("scalar", lambda E, hp=hp, bank=bank, dst=fam_dst[fname]:
                            E.activation(
                        out=bv(dst + hp * 256, 512), in_=psum[bank][:],
                        func=AF.Exp), wait=wexp, inc=sss, dr=True)
                    expd[hp] = sss.v
                exprdy = sss.now
                dst = fam_dst[fname]
                for ch in range(8):
                    wsum = [exprdy]
                    if ch == 0:
                        wsum.append((svv, svv.v))
                    pb.emit("tensor", lambda E, ch=ch, dst=dst: E.matmul(
                        psum[2][0:2, :], brep2t[:], bv(dst + ch * 256, 512),
                        start=True, stop=True), wait=wsum, inc=smm)
                    pb.emit("vector", lambda E: E.reciprocal(
                        out=fv(L_RSF, 512)[0:2, :],
                        in_=psum[2][0:2, :]), wait=[smm.now], inc=svv, dr=True)
                    pb.emit("vector", lambda E: E.tensor_copy(
                        out=bv(L_T3 + 512, 512)[0:2, :],
                        in_=fv(L_RSF, 512)[0:2, :]), inc=svv, dr=True)
                    pb.emit("tensor", lambda E: E.matmul(
                        psum[3][:], brep2[:],
                        bv(L_T3 + 512, 512)[0:2, :],
                        start=True, stop=True), wait=[svv.now], inc=smm)
                    pb.emit("vector", lambda E, ch=ch, dst=dst: E.tensor_mul(
                        out=bv(dst + ch * 256, 512), in0=bv(dst + ch * 256, 512),
                        in1=psum[3][:]), wait=[smm.now], inc=svv, dr=True)
            smdone = svv.now
            if dbg and t == 0:
                pb.dma("sync", dbg_sm_h.ap()[0], ARb[:, 2 * L_YSM:2 * L_YSM + 4096],
                       sst, wait=[smdone])
                pb.dma("sync", dbg_sm_h.ap()[1], ARb[:, 2 * L_SMQ:2 * L_SMQ + 4096],
                       sst)
                pb.dma("sync", dbg_sm_h.ap()[2], ARb[:, 2 * L_SMK:2 * L_SMK + 4096],
                       sst)
                pb.dma("sync", dbg_sm_h.ap()[3], ARb[:, 2 * L_XTOK:2 * L_XTOK + 4096],
                       sst)

            if t > 0:
                emit_composite(3, t)
            xbrdy = sev.now
            for hp in range(8):
                if t > 0:
                    rhs = bv(L_XD + hp * 512, 512)
                else:
                    rhs = ARb[:, hp * TPC + t * 512:hp * TPC + (t + 1) * 512]
                pb.emit("tensor", lambda E, hp=hp, rhs=rhs: E.matmul(
                    psum[4][0:8, :], wbbd_sb[:, hp * 8:(hp + 1) * 8], rhs,
                    start=True, stop=True), wait=[wbdrdy, xbrdy, svv.now],
                    inc=smm)
                pb.emit("scalar", lambda E: E.activation(
                    out=fv(L_T2, 512)[0:8, :], in_=psum[4][0:8, :],
                    func=AF.Tanh, scale=0.5), wait=[smm.now], inc=sss, dr=True)
                pb.emit("vector", lambda E: E.tensor_scalar(
                    out=fv(L_T2, 512)[0:8, :], in0=fv(L_T2, 512)[0:8, :],
                    scalar1=0.5, scalar2=0.5, op0=ALU.mult, op1=ALU.add),
                    wait=[sss.now], inc=svv, dr=True)
                pb.emit("vector", lambda E: E.tensor_copy(
                    out=bv(L_T3, 512)[0:8, :], in_=fv(L_T2, 512)[0:8, :]),
                    inc=svv, dr=True)
                vbeta = svv.now
                for jt in range(4):
                    pb.emit("tensor", lambda E, jt=jt: E.transpose(
                        pst[1][:, 0:8], bv(L_T3 + jt * 64, 128)[0:8, :],
                        ident_bf[0:8, 0:8]),
                        wait=[vbeta, (sev, sev.v)], inc=smm)
                    pb.emit("vector", lambda E, jt=jt, hp=hp: E.tensor_copy(
                        out=bass.AP(AR[:].tensor, L_BJ + jt * 64 + 8 * hp,
                                    [[ARN, P], [4, 2], [1, 4]]),
                        in_=pst[1][:, 0:8].rearrange("p (a b) -> p a b", a=2)),
                        wait=[smm.now], inc=sev)
            betardy = sev.now

            for ch in range(8):
                pb.emit("vector", lambda E, ch=ch: E.tensor_sub(
                    out=bv(L_SMQ + ch * 256, 512), in0=bv(L_SMQ + ch * 256, 512),
                    in1=bv(L_SMK + ch * 256, 512)),
                    wait=[smdone], inc=svv, dr=True)
            qmkv = svv.now
            if t < 3:
                for src, dstb in ((L_SMK, L_KS[t]), (L_SMQ, L_QMK[t])):
                    for hp in range(8):
                        for jt in range(4):
                            bank = (hp * 4 + jt) % 2
                            pb.emit("tensor", lambda E, src=src, hp=hp, jt=jt,
                                    bank=bank: E.transpose(
                                        pst[bank][:, 0:P],
                                        bv(src + hp * 256 + jt * 64, 128),
                                        ident_bf[:]),
                                    wait=[qmkv, (sev, sev.v - (0 if
                                          (hp * 4 + jt) < 2 else 1))],
                                    inc=smm)
                            pb.emit("vector", lambda E, dstb=dstb, hp=hp, jt=jt,
                                    bank=bank: E.tensor_copy(
                                        out=bass.AP(ARb.tensor,
                                                    2 * dstb + jt * 1024 +
                                                    128 * hp,
                                                    [[2 * ARN, P], [64, 2],
                                                     [1, 64]]),
                                        in_=pst[bank][:, 0:P].rearrange(
                                            "p (a b) -> p a b", a=2)),
                                    wait=[smm.now], inc=sev)
                kqdone = sev.now
                for s in range(t):
                    for jt in range(4):
                        pb.emit("vector", lambda E, s=s, jt=jt, t=t:
                                E.tensor_mul(
                                    out=fv(L_T1, 1024),
                                    in0=bv(L_KS[s] + jt * 512, 1024),
                                    in1=bv(L_QMK[t] + jt * 512, 1024)),
                                wait=[kqdone], inc=svv, dr=True)
                        pb.emit("vector", lambda E, s=s, jt=jt: E.tensor_reduce(
                            out=fv(C_D + s * 64 + jt * 16, 16),
                            in_=hview(L_T1), axis=AX.X, op=ALU.add),
                            inc=svv, dr=True)
                for fi in range(4):
                    for r in range(t):
                        first = True
                        for s in range(r, t):
                            sl = sridx[(s, r)]
                            if first:
                                pb.emit("vector", lambda E, fi=fi, sl=sl, s=s:
                                        E.tensor_mul(
                                            out=scl3(WSC),
                                            in0=scl3(TT_D + (fi * 6 + sl) * 64),
                                            in1=scl3(C_D + s * 64)),
                                        wait=[betardy], inc=svv, dr=True)
                                first = False
                            else:
                                pb.emit("vector", lambda E, fi=fi, sl=sl, s=s:
                                        E.tensor_mul(
                                            out=scl3(WSC + 64),
                                            in0=scl3(TT_D + (fi * 6 + sl) * 64),
                                            in1=scl3(C_D + s * 64)),
                                        inc=svv, dr=True)
                                pb.emit("vector", lambda E: E.tensor_add(
                                    out=scl3(WSC), in0=scl3(WSC),
                                    in1=scl3(WSC + 64)), inc=svv, dr=True)
                        sl = sridx[(t, r)]
                        pb.emit("vector", lambda E, fi=fi, sl=sl: E.tensor_mul(
                            out=scl3(TT_D + (fi * 6 + sl) * 64),
                            in0=scl3(WSC), in1=bj3(fi)),
                            inc=svv, dr=True)
                    sl = sridx[(t, t)]
                    pb.emit("vector", lambda E, fi=fi, sl=sl: E.tensor_copy(
                        out=scl3(TT_D + (fi * 6 + sl) * 64), in_=bj3(fi)),
                        wait=[betardy], inc=svv, dr=True)

            if dbg and t == 0:
                pb.dma("sync", dbg_tb_h.ap()[:, 0:2048], fv(SCL, 2048), sst,
                       wait=[(svv, svv.v), (sev, sev.v)])
                pb.dma("sync", dbg_tb_h.ap()[:, 2048:2304], fv(L_BJ, 256), sst)

            allb = [psum[i][:] for i in range(6)] + [pstf[0], pstf[1]]
            owl = {}
            for hp in range(8):
                buf = L_XD + (hp % 2) * 512
                w = [smdone]
                if hp == 0:
                    w.append((smm, smm.v))
                if hp >= 2:
                    w.append((smm, owl[hp - 2]))
                pb.emit("gpsimd", lambda E, hp=hp, buf=buf: E.dma_start(
                    out=bv(buf, 1024),
                    in_=outwT_h.ap()[hp * P:(hp + 1) * P, :]),
                    wait=w, inc=ldg, incn=16)
                owrdy = ldg.now

                def owmm(E, hp=hp, buf=buf, t=t):
                    for js in range(4):
                        E.matmul(allb[2 * js],
                                 ARb[:, 2 * L_YSM + hp * 512 + js * P:
                                     2 * L_YSM + hp * 512 + (js + 1) * P],
                                 bv(buf, 512), start=(hp == 0), stop=(hp == 7))
                        ins = E.matmul(allb[2 * js + 1],
                                       ARb[:, 2 * L_YSM + hp * 512 + js * P:
                                           2 * L_YSM + hp * 512 + (js + 1) * P],
                                       bv(buf + 256, 512),
                                       start=(hp == 0), stop=(hp == 7))
                    return ins
                w2_ = [owrdy]
                if hp == 0:
                    w2_.append((sev, sev.v))
                if t > 0 and hp == 0:
                    w2_.append((svv, st_out[t - 1]))
                pb.emit("tensor", owmm, wait=w2_, inc=smm)
                owl[hp] = smm.v
            owdone = smm.now
            for js in range(4):
                tt = t * 4 + js
                pb.emit("sync", lambda E, tt=tt: E.dma_start(
                    out=fv(L_T1, D), in_=x1_spill_h.ap()[tt]),
                    wait=[(svv, svv.v)], inc=ld, incn=16)
                pb.emit("vector", lambda E, tt=tt: E.tensor_scalar(
                    out=fv(L_T2, D), in0=fv(L_T1, D),
                    scalar1=stats1[:, 4 * tt + 2:4 * tt + 3],
                    scalar2=stats1[:, 4 * tt + 3:4 * tt + 4],
                    op0=ALU.mult, op1=ALU.add), wait=[ld.now], inc=svv, dr=True)
                pb.emit("vector", lambda E: E.tensor_mul(
                    out=fv(L_T2, D), in0=fv(L_T2, D), in1=lng_rep[:]),
                    inc=svv, dr=True)
                pb.emit("vector", lambda E: E.tensor_add(
                    out=fv(L_T2, D), in0=fv(L_T2, D), in1=lnb_rep[:]),
                    inc=svv, dr=True)
                if dbg and tt == 0:
                    pb.dma("sync", dbg_cb_h.ap()[0], fv(L_T2, D), sst,
                           wait=[(svv, svv.v)])
                    pb.emit("vector", lambda E: E.tensor_copy(
                        out=fv(L_T3, 512), in_=allb[0]), wait=[owdone],
                        inc=svv, dr=True)
                    pb.emit("vector", lambda E: E.tensor_copy(
                        out=fv(L_T3 + 512, 512), in_=allb[1]), inc=svv, dr=True)
                    pb.dma("sync", dbg_cb_h.ap()[1], fv(L_T3, D), sst,
                           wait=[(svv, svv.v)])
                pb.emit("vector", lambda E, js=js: E.tensor_add(
                    out=fv(L_T3, 512), in0=fv(L_T2, 512), in1=allb[2 * js]),
                    wait=[owdone], inc=svv, dr=True)
                pb.emit("vector", lambda E, js=js: E.tensor_add(
                    out=fv(L_T3 + 512, 512), in0=fv(L_T2 + 512, 512),
                    in1=allb[2 * js + 1]), inc=svv, dr=True)
                pb.emit("vector", lambda E, tt=tt: E.tensor_copy(
                    out=fv(L_T2, D),
                    in_=ARb[:, GS2 + tt * 1024:GS2 + (tt + 1) * 1024]),
                    inc=svv, dr=True)
                pb.emit("vector", lambda E: E.tensor_mul(
                    out=fv(L_T3, D), in0=fv(L_T3, D), in1=fv(L_T2, D)),
                    inc=svv, dr=True)
                pb.emit("vector", lambda E: E.tensor_add(
                    out=fv(L_T2, D), in0=fv(L_T1, D), in1=fv(L_T3, D)),
                    inc=svv, dr=True)
                if dbg and tt == 0:
                    pb.dma("sync", dbg_cb_h.ap()[2], fv(L_T2, D), sst,
                           wait=[(svv, svv.v)])
                vout = svv.now
                bq, sub = tt // 4, tt % 4
                pb.emit("sync", lambda E, bq=bq, sub=sub: E.dma_start(
                    out=out_h.ap()[bq, sub * P:(sub + 1) * P, :], in_=fv(L_T2, D)),
                    wait=[vout], inc=sst, incn=16)
            st_out[t] = svv.v

        pb.wait("sync", (sst, sst.v))

        block = es.enter_context(nc.Block())
        pb.replay_into(block)
    return nc


def host_inputs(inp, core):
    asf = np.ascontiguousarray
    x = np.asarray(inp["L0_attn_x"])
    d = {}
    d["x"] = asf(x[:, core * TW:(core + 1) * TW, :])
    d["wiT"] = asf(np.asarray(inp["L0_attn_Wi"]).T)
    d["bi128"] = asf(np.asarray(inp["L0_attn_bi"]).reshape(32, P).T)
    d["woT"] = asf(np.asarray(inp["L0_attn_Wo"]).T)
    d["bo_row"] = asf(np.asarray(inp["L0_attn_bo"]).reshape(1, D))
    d["dv128"] = asf(np.asarray(inp["L0_attn_Dv"]).reshape(16, P).T)
    d["wgT"] = asf(np.asarray(inp["L0_ffn_w_gate"]).T)
    d["wuT"] = asf(np.asarray(inp["L0_ffn_w_up"]).T)
    d["wdT"] = asf(np.asarray(inp["L0_ffn_w_down"]).T)
    sl = slice(core * DSL, (core + 1) * DSL)

    def r128(a):
        a = np.asarray(a)
        return asf(a[sl].reshape(2, P, -1).transpose(1, 0, 2).reshape(P, -1))
    d["alog"] = r128(inp["L0_attn_A_log"])
    d["bm"] = r128(inp["L0_attn_Bm"])
    d["cm"] = r128(inp["L0_attn_Cm"])
    d["cw"] = r128(np.asarray(inp["L0_attn_conv_w"]).reshape(DI, 4))
    d["cb"] = r128(np.asarray(inp["L0_attn_conv_b"]).reshape(DI, 1))
    d["wdtT"] = asf(np.asarray(inp["L0_attn_Wdt"])[sl].T)
    d["bdt128"] = r128(np.asarray(inp["L0_attn_bdt"]).reshape(DI, 1))
    Wy = np.asarray(inp["L1_attn_W_y"])[0]
    Wq = np.asarray(inp["L1_attn_W_q"])[0]
    Wk = np.asarray(inp["L1_attn_W_k"])[0]
    wb = np.asarray(inp["L1_attn_w_b"])[0]

    def bd(W):
        o = np.zeros((8, P, P), np.float32)
        for hp in range(8):
            o[hp, 0:64, 0:64] = W[2 * hp].T
            o[hp, 64:128, 64:128] = W[2 * hp + 1].T
        return o
    d["wy_bd"] = bd(Wy)
    d["wq_bd"] = bd(Wq)
    d["wk_bd"] = bd(Wk)
    o = np.zeros((8, P, 8), np.float32)
    for hp in range(8):
        o[hp, 0:64, 0:4] = wb[2 * hp]
        o[hp, 64:128, 4:8] = wb[2 * hp + 1]
    d["wb_bd"] = o
    d["outwT"] = asf(np.asarray(inp["L1_attn_out_w"]).T)
    d["lng_row"] = asf(np.asarray(inp["L1_attn_ln_g"]).reshape(1, D))
    d["lnb_row"] = asf(np.asarray(inp["L1_attn_ln_b"]).reshape(1, D))
    d["w1"] = asf(np.asarray(inp["L1_ffn_w1"]))
    d["b1128"] = asf(np.asarray(inp["L1_ffn_b1"]).reshape(4, P).T)
    d["w2"] = asf(np.asarray(inp["L1_ffn_w2"]))
    d["b2_row"] = asf(np.asarray(inp["L1_ffn_b2"]).reshape(1, 8))
    d["eW"] = asf(np.asarray(inp["L1_ffn_expert_W"]))
    d["eb"] = asf(np.asarray(inp["L1_ffn_expert_b"]).reshape(8, 1, D))
    d["ident128"] = np.eye(P, dtype=np.float32)
    br = np.zeros((2, P), np.float32)
    br[0, 0:64] = 1.0
    br[1, 64:128] = 1.0
    d["brep2h"] = br
    d["brep2th"] = asf(br.T)
    d["lngrep"] = asf(np.broadcast_to(
        np.asarray(inp["L1_attn_ln_g"]).reshape(1, D), (P, D)))
    d["lnbrep"] = asf(np.broadcast_to(
        np.asarray(inp["L1_attn_ln_b"]).reshape(1, D), (P, D)))
    return d


def kernel(**inputs):
    nc = build_program(dbg=0)
    in_maps = [host_inputs(inputs, c) for c in range(NC)]
    res = run_bass_kernel_spmd(nc, in_maps, core_ids=list(range(NC)))
    out = np.concatenate([res.results[c]["out"] for c in range(NC)], axis=1)
    return out.astype(np.float32)
